# version 16
# baseline (speedup 1.0000x reference)
import numpy as np
from contextlib import ExitStack

import concourse.bass as bass
import concourse.mybir as mybir
from concourse.bass_utils import run_bass_kernel_spmd

F32 = mybir.dt.float32
BF16 = mybir.dt.bfloat16
I32 = mybir.dt.int32
AF = mybir.ActivationFunctionType
ALU = mybir.AluOpType

T = 2048
D = 1024
NCHUNK = 8
NSPAN = 4
SPW = 512
EPS = 1e-5
C0 = 4.0
C1 = 12.0
QSCALE0 = 0.125
TANH_SC = 0.125 / 50.0
DEQ = 1.0 / 127.0
R_SUM, R_SQ, R_VAR = 0, 32, 64


class Bld:
    def __init__(self, nc, block, es):
        self.nc = nc
        self.sec = {"T": block.tensor, "S": block.scalar, "V": block.vector,
                    "G": block.gpsimd, "Y": block.sync}
        self.sem = {}
        self.seen = {}
        self.buf = {"T": [], "S": [], "V": [], "G": [], "Y": []}
        self.owner = {"semT": "T", "semS": "S", "semV": "V", "semG": "G"}
        for name in ("semT", "semS", "semV", "semG", "dml", "dms"):
            h = es.enter_context(nc.semaphore(name))
            self.sem[name] = [h, 0]

    def n(self, name):
        return self.sem[name][1]

    def emit(self, eng, f, inc=None, k=1, waits=()):
        real = []
        for w in waits:
            if w is None:
                continue
            sname, val = w
            if val is None or val <= 0:
                continue
            if self.owner.get(sname) == eng:
                continue
            key = (eng, sname)
            if self.seen.get(key, 0) >= val:
                continue
            self.seen[key] = val
            real.append((self.sem[sname][0], val))
        incsem = None
        if inc:
            self.sem[inc][1] += k
            incsem = (self.sem[inc][0], k)

        def body(e):
            for h, v in real:
                e.wait_ge(h, v)
            ins = f(e)
            if incsem is not None:
                if isinstance(ins, (list, tuple)):
                    ins = ins[-1]
                ins.then_inc(*incsem)
        self.buf[eng].append(body)
        return self.sem[inc][1] if inc else None

    def flush(self):
        for eng, bodies in self.buf.items():
            if bodies:
                bs = list(bodies)
                bodies.clear()

                def runner(e, bs=bs):
                    for b in bs:
                        b(e)
                self.sec[eng](runner)


class _StopEmit(Exception):
    pass


def build_nc(tap=None, upto=None):
    nc = bass.Bass("TRN2", target_bir_lowering=False,
                   detect_race_conditions=False)

    xT = nc.dram_tensor("xT", [D, T], F32, kind="ExternalInput").ap()
    wq0 = nc.dram_tensor("wq0", [D, D], F32, kind="ExternalInput").ap()
    wk0 = nc.dram_tensor("wk0", [D, D], F32, kind="ExternalInput").ap()
    wv0 = nc.dram_tensor("wv0", [D, D], F32, kind="ExternalInput").ap()
    wo0 = nc.dram_tensor("wo0", [D, D], F32, kind="ExternalInput").ap()
    vecs_h = nc.dram_tensor("vecs", [128, 48], F32, kind="ExternalInput").ap()
    rows_h = nc.dram_tensor("rows4", [1, D], F32, kind="ExternalInput").ap()
    wg_h = nc.dram_tensor("wg", [D, 3072], I32, kind="ExternalInput").ap()
    wu_h = nc.dram_tensor("wu", [D, 3072], I32, kind="ExternalInput").ap()
    wd_h = nc.dram_tensor("wd", [3072, D], I32, kind="ExternalInput").ap()
    wq1 = nc.dram_tensor("wq1", [D, D], F32, kind="ExternalInput").ap()
    wk1 = nc.dram_tensor("wk1", [D, D], F32, kind="ExternalInput").ap()
    wv1 = nc.dram_tensor("wv1", [D, D], F32, kind="ExternalInput").ap()
    wo1 = nc.dram_tensor("wo1", [D, D], F32, kind="ExternalInput").ap()
    wr1 = nc.dram_tensor("wr1", [D, D], F32, kind="ExternalInput").ap()
    wk1f = nc.dram_tensor("wk1f", [D, D], F32, kind="ExternalInput").ap()
    wv1f = nc.dram_tensor("wv1f", [D, D], F32, kind="ExternalInput").ap()
    out_h = nc.dram_tensor("out", [D, T], F32, kind="ExternalOutput").ap()
    denh = nc.dram_tensor("denh", [16, T], BF16, kind="Internal").ap()
    cpats = nc.dram_tensor("cpats", [16, 1024], F32, kind="ExternalInput").ap()
    cmsk = nc.dram_tensor("cmsk", [128, 1024], F32, kind="ExternalInput").ap()
    cpsw = nc.dram_tensor("cpsw", [128, 128], F32, kind="ExternalInput").ap()
    cpos = nc.dram_tensor("cpos", [128, T], F32, kind="ExternalInput").ap()
    cdd = nc.dram_tensor("cdd", [128, 2], F32, kind="ExternalInput").ap()
    tap_h = None
    if tap is not None:
        tap_h = nc.dram_tensor("tap", [D, T], F32, kind="ExternalOutput").ap()

    es = ExitStack()
    A = es.enter_context

    X = A(nc.sbuf_tensor("X", [128, NCHUNK, T], BF16))
    XT_ = A(nc.sbuf_tensor("XT", [128, NCHUNK, T], BF16))
    STAT = A(nc.sbuf_tensor("STAT", [65, T], F32))
    RECB = A(nc.sbuf_tensor("RECB", [65, T], BF16))
    PT = A(nc.sbuf_tensor("PT", [128, 2, 2, SPW], BF16))
    MB = PT[:, 0, :, :]
    RB = PT[:, 1, :, :]
    VEC = A(nc.sbuf_tensor("VEC", [128, 48], F32))
    BVROW = A(nc.sbuf_tensor("BVROW", [1, D], BF16))
    ONE1 = A(nc.sbuf_tensor("ONE1", [65, 128], BF16))
    ONEC = A(nc.sbuf_tensor("ONEC", [128, 1], BF16))
    CB = A(nc.sbuf_tensor("CB", [128, 4], F32))
    PATS = A(nc.sbuf_tensor("PATS", [16, NCHUNK, 128], BF16))
    DROW = A(nc.sbuf_tensor("DROW", [33, 2, 512], BF16))
    MSK = A(nc.sbuf_tensor("MSK", [128, 1024], BF16))
    ATMP = A(nc.sbuf_tensor("ATMP", [128, T], BF16))
    WSL = A(nc.sbuf_tensor("WSL", [128, 2, NCHUNK, 128], BF16))
    X2S = A(nc.sbuf_tensor("X2S", [128, 2, SPW], BF16))

    PSA = A(nc.psum_tensor("PSA", [128, 1536], F32))
    PSB = A(nc.psum_tensor("PSB", [128, 1536], F32))
    PSC = A(nc.psum_tensor("PSC", [128, 512], F32))
    PSD = A(nc.psum_tensor("PSD", [128, 512], F32))

    blk = nc.Block()
    block = blk.__enter__()
    B = Bld(nc, block, es)

    def dma(out, in_):
        return lambda g: g.dma_start(out=out, in_=in_)

    def phase_sync():
        B.flush()
        B.emit("T", lambda t: t.nop(), inc="semT",
               waits=[("semS", B.n("semS")), ("semV", B.n("semV")),
                      ("semG", B.n("semG"))])

    def now_sv():
        return [("semS", B.n("semS")), ("semV", B.n("semV"))]

    stop_state = []
    open_scopes = []

    def ck(stage, buf):
        if upto == stage:
            stop_state.append((buf, B.n("semV"), B.n("semS")))
            raise _StopEmit()

    trig_es = ExitStack()
    open_scopes.append(trig_es)
    COSb = trig_es.enter_context(nc.sbuf_tensor("COSb", [128, T], BF16))
    SINSb = trig_es.enter_context(nc.sbuf_tensor("SINSb", [128, T], BF16))
    PSW = trig_es.enter_context(nc.sbuf_tensor("PSW", [128, 128], BF16))
    init_es = ExitStack()
    Ai = init_es.enter_context
    ROWS = Ai(nc.sbuf_tensor("ROWS", [1, D], F32))
    ITMP = Ai(nc.sbuf_tensor("ITMP", [128, T], I32))
    FTI = Ai(nc.sbuf_tensor("FTI", [128, T], F32))
    FTI2 = Ai(nc.sbuf_tensor("FTI2", [128, T], F32))
    FTIP = Ai(nc.sbuf_tensor("FTIP", [128, T], F32))
    DD = Ai(nc.sbuf_tensor("DD", [128, 6], F32))
    CDD = Ai(nc.sbuf_tensor("CDD", [128, 2], F32))

    B.emit("G", dma(X[:, :, :], xT.rearrange("(c p) t -> p c t", p=128)),
           inc="dml", k=16)
    x_loaded = B.n("dml")
    B.emit("G", dma(VEC[:, :], vecs_h), inc="dml", k=16)
    B.emit("G", dma(ROWS[:, :], rows_h), inc="dml", k=16)
    vec_loaded = B.n("dml")

    B.emit("G", dma(MSK[:, :], cmsk), inc="dml", k=16)
    B.emit("G", dma(PATS[:, :, :].rearrange("k a j -> k (a j)"), cpats),
           inc="dml", k=16)
    B.emit("G", dma(PSW[:, :], cpsw), inc="dml", k=16)
    B.emit("G", dma(FTIP[:, :], cpos), inc="dml", k=16)
    B.emit("G", dma(CDD[:, :], cdd), inc="dml", k=16)
    consts_dma = B.n("dml")
    B.emit("V", lambda v: v.memset(ONE1[0:1, :], 1.0), inc="semV")
    B.emit("V", lambda v: v.memset(ONE1[32:33, :], 1.0), inc="semV")
    B.emit("V", lambda v: v.memset(ONE1[64:65, :], 1.0), inc="semV")
    B.emit("V", lambda v: v.memset(ONEC[:, :], 1.0), inc="semV")
    B.emit("V", lambda v: v.memset(CB[:, 0:1], EPS), inc="semV")
    B.emit("V", lambda v: v.memset(CB[:, 1:2], -C0), inc="semV")
    B.emit("V", lambda v: v.memset(CB[:, 2:3], -C1), inc="semV")
    B.emit("V", lambda v: v.memset(RECB[:, :], 0.0), inc="semV")
    B.emit("V", lambda v: v.tensor_copy(BVROW[0:1, :], ROWS[0:1, :]),
           inc="semV", waits=[("dml", vec_loaded)])
    consts_v = B.n("semV")

    LN1E4 = float(np.log(10000.0))
    B.emit("S", lambda s: s.activation(DD[:, 2:3], CDD[:, 0:1], AF.Exp,
                                       scale=-LN1E4 / 32.0),
           inc="semS", waits=[("dml", consts_dma)])
    th_done = B.n("semS")
    INV2PI = float(1.0 / (2 * np.pi))
    TWOPI = float(2 * np.pi)

    def rope_trig(out_bf, quarter_shift, sign_mult):
        B.emit("V", lambda v: v.tensor_scalar(FTI[:, :], FTIP[:, :],
                                              DD[:, 2:3], None, op0=ALU.mult),
               inc="semV",
               waits=[("semS", th_done), ("dml", consts_dma)])
        B.emit("V", lambda v: v.tensor_scalar(FTI[:, :], FTI[:, :], INV2PI,
                                              quarter_shift, op0=ALU.mult,
                                              op1=ALU.add), inc="semV")
        B.emit("V", lambda v: v.tensor_copy(ITMP[:, :], FTI[:, :]),
               inc="semV")
        B.emit("V", lambda v: v.tensor_tensor(FTI[:, :], FTI[:, :],
                                              ITMP[:, :], op=ALU.subtract),
               inc="semV")
        B.emit("V", lambda v: v.tensor_scalar(FTI2[:, :], FTI[:, :], 0.5,
                                              None, op0=ALU.is_ge),
               inc="semV")
        B.emit("V", lambda v: v.tensor_tensor(FTI[:, :], FTI[:, :],
                                              FTI2[:, :], op=ALU.subtract),
               inc="semV")
        B.emit("V", lambda v: v.tensor_scalar(FTI2[:, :], FTI[:, :], -0.5,
                                              None, op0=ALU.is_lt),
               inc="semV")
        B.emit("V", lambda v: v.tensor_tensor(FTI[:, :], FTI[:, :],
                                              FTI2[:, :], op=ALU.add),
               inc="semV")
        B.emit("V", lambda v: v.tensor_scalar(FTI[:, :], FTI[:, :], TWOPI,
                                              None, op0=ALU.mult),
               inc="semV")
        n = B.n("semV")
        B.emit("S", lambda s: s.activation(FTI2[:, :], FTI[:, :], AF.Sin),
               inc="semS", waits=[("semV", n)])
        n2 = B.n("semS")
        if sign_mult:
            B.emit("V", lambda v: v.tensor_scalar(out_bf[:, :], FTI2[:, :],
                                                  CDD[:, 1:2], None,
                                                  op0=ALU.mult),
                   inc="semV", waits=[("semS", n2)])
        else:
            B.emit("V", lambda v: v.tensor_copy(out_bf[:, :], FTI2[:, :]),
                   inc="semV", waits=[("semS", n2)])

    rope_trig(COSb, 0.25, False)
    rope_trig(SINSb, 0.0, True)
    consts_done = B.n("semV")

    def ln_core(src, writeback, src_marks=None, sum_row=R_SUM,
                skip_sum=False):
        wv0 = B.n("semV")
        ws0 = B.n("semS")
        x2marks = {}
        evmarks = {}
        wbmark = {0: None, 1: None}
        evmark = {0: None, 1: None}
        span_marks = []
        for s in range(NSPAN):
            par = s % 2
            sm = src_marks[s] if src_marks is not None else ("semV", wv0)
            for c in range(NCHUNK):
                i = (s * NCHUNK + c) % 2
                B.emit("S", lambda sc, s=s, c=c, i=i: sc.activation(
                    X2S[:, i, :],
                    src[:, c, s * SPW:(s + 1) * SPW], AF.Square),
                    inc="semS",
                    waits=[("semT", x2marks.get(s * NCHUNK + c - 2)),
                           sm, ("dml", x_loaded), ("semS", ws0)])
                nsq = B.n("semS")
                if not skip_sum:
                    B.emit("T", lambda t, s=s, c=c, par=par: t.matmul(
                        PSA[0:1, par * SPW:(par + 1) * SPW], ONEC[:, 0:1],
                        src[:, c, s * SPW:(s + 1) * SPW],
                        start=(c == 0), stop=(c == NCHUNK - 1)),
                        inc="semT",
                        waits=([("semV", consts_done), sm,
                                ("semV", evmarks.get(s - 2))]
                               if c == 0 else ()))
                B.emit("T", lambda t, s=s, c=c, i=i, par=par: t.matmul(
                    PSB[0:1, par * SPW:(par + 1) * SPW], ONEC[:, 0:1],
                    X2S[:, i, :],
                    start=(c == 0), stop=(c == NCHUNK - 1)),
                    inc="semT", waits=[("semS", nsq)])
                x2marks[s * NCHUNK + c] = B.n("semT")
            nsp = B.n("semT")
            sl = slice(s * SPW, (s + 1) * SPW)
            if not skip_sum:
                B.emit("V", lambda v, sl=sl, par=par, sum_row=sum_row:
                       v.tensor_copy(
                    STAT[sum_row:sum_row + 1, sl],
                    PSA[0:1, par * SPW:(par + 1) * SPW]),
                    inc="semV", waits=[("semT", nsp)])
            B.emit("V", lambda v, sl=sl, sum_row=sum_row: v.tensor_tensor(
                STAT[R_VAR:R_VAR + 1, sl], STAT[sum_row:sum_row + 1, sl],
                STAT[sum_row:sum_row + 1, sl], op=ALU.mult),
                inc="semV", waits=[("semT", nsp)])
            B.emit("V", lambda v, sl=sl, par=par: v.scalar_tensor_tensor(
                STAT[R_VAR:R_VAR + 1, sl], STAT[R_VAR:R_VAR + 1, sl],
                -1.0 / D, PSB[0:1, par * SPW:(par + 1) * SPW],
                op0=ALU.mult, op1=ALU.add), inc="semV")
            evmarks[s] = B.n("semV")
            B.emit("V", lambda v, sl=sl, sum_row=sum_row: v.tensor_scalar(
                RECB[32:33, sl], STAT[sum_row:sum_row + 1, sl],
                1.0 / D, None, op0=ALU.mult), inc="semV")
            nrow = B.n("semV")
            B.emit("S", lambda sc, sl=sl: sc.activation(
                STAT[R_VAR:R_VAR + 1, sl], STAT[R_VAR:R_VAR + 1, sl], AF.Ln,
                scale=1.0 / D, bias=CB[64:65, 0:1]),
                inc="semS", waits=[("semV", nrow)])
            B.emit("S", lambda sc, sl=sl: sc.activation(
                RECB[64:65, sl], STAT[R_VAR:R_VAR + 1, sl], AF.Exp,
                scale=-0.5), inc="semS")
            nr = B.n("semS")
            psm = PSA[:, 1024:1536]
            psr = PSB[:, 1024:1536]
            B.emit("T", lambda t, sl=sl, psm=psm: t.matmul(
                psm, ONE1[32:33, :], RECB[32:33, sl],
                start=True, stop=True),
                inc="semT",
                waits=[("semV", nrow), ("semS", nr), evmark[0]])
            n1 = B.n("semT")
            B.emit("T", lambda t, sl=sl, psr=psr: t.matmul(
                psr, ONE1[64:65, :], RECB[64:65, sl],
                start=True, stop=True), inc="semT")
            n2 = B.n("semT")
            B.emit("S", lambda sc, par=par, psm=psm: sc.activation(
                MB[:, par, :], psm, AF.Identity),
                inc="semS", waits=[("semT", n1), wbmark[par]])
            B.emit("S", lambda sc, par=par, psr=psr: sc.activation(
                RB[:, par, :], psr, AF.Identity),
                inc="semS", waits=[("semT", n2)])
            nbc = B.n("semS")
            evmark[0] = ("semS", nbc)
            for c in range(NCHUNK):
                writeback(c, s, par, nbc)
            wbmark[par] = ("semV", B.n("semV"))
            span_marks.append(B.n("semV"))
        return span_marks

    def stream_ln(src, dst, src_marks=None, skip_sum=False):
        def wb(c, s, par, nbc):
            sl = slice(s * SPW, (s + 1) * SPW)
            B.emit("V", lambda v, c=c, sl=sl, par=par: v.tensor_tensor(
                ATMP[:, sl], src[:, c, sl], MB[:, par, :], op=ALU.subtract),
                inc="semV", waits=[("semS", nbc)])
            B.emit("V", lambda v, c=c, sl=sl, par=par: v.tensor_tensor(
                dst[:, c, sl], ATMP[:, sl], RB[:, par, :], op=ALU.mult),
                inc="semV")
        phase_sync()
        return ln_core(src, wb, src_marks, sum_row=R_SQ,
                       skip_sum=skip_sum)

    def inner_ln_residual(y, src_marks=None):
        def wb(c, s, par, nbc):
            sl = slice(s * SPW, (s + 1) * SPW)
            B.emit("V", lambda v, c=c, sl=sl, par=par: v.tensor_tensor(
                ATMP[:, sl], y[:, c, sl], MB[:, par, :], op=ALU.subtract),
                inc="semV", waits=[("semS", nbc)])
            B.emit("V", lambda v, c=c, sl=sl, par=par: v.tensor_tensor(
                ATMP[:, sl], ATMP[:, sl], RB[:, par, :], op=ALU.mult),
                inc="semV")
            B.emit("V", lambda v, c=c, sl=sl: v.tensor_tensor(
                X[:, c, sl], X[:, c, sl], ATMP[:, sl], op=ALU.add),
                inc="semV")
        phase_sync()
        return ln_core(y, wb, src_marks)

    def projection(w_hbm, src, name, evac=None, moving_ready=None,
                   bias_col=None, act=None, dst=None, evac_waits=(),
                   mm_first_waits=(), dma_guard=False, no_sync=False,
                   spanm_out=None):
        if not no_sync:
            phase_sync()
        w_r = w_hbm.rearrange("(c p) f -> p c f", p=128)
        tprev = B.n("semT")
        tmarks = {}
        bankmark = {0: None, 1: None}
        first = True
        last = None
        for f in range(NCHUNK):
            dw = [("semT", tmarks.get(f - 2, tprev))]
            if f == 0 and dma_guard:
                dw += now_sv()
            B.emit("G", dma(WSL[:, f % 2, :, :],
                            w_r[:, :, f * 128:(f + 1) * 128]),
                   inc="dml", k=16, waits=dw)
            ndma = B.n("dml")
            for s in range(NSPAN):
                bk = s % 2
                ps = PSC[:, :] if bk == 0 else PSD[:, :]
                wts = [("dml", ndma), bankmark[bk]]
                if moving_ready is not None:
                    mv = (moving_ready[s] if isinstance(moving_ready, list)
                          else moving_ready)
                    wts.append(("semV", mv))
                if first:
                    wts += list(mm_first_waits)
                    first = False
                for c in range(NCHUNK):
                    B.emit("T", lambda t, f=f, s=s, c=c, ps=ps: t.matmul(
                        ps, WSL[:, f % 2, c, :],
                        src[:, c, s * SPW:(s + 1) * SPW],
                        start=(c == 0), stop=(c == NCHUNK - 1)),
                        inc="semT", waits=wts if c == 0 else ())
                nmm = B.n("semT")
                if evac is not None:
                    mk = evac(f, s, ps, nmm)
                    bankmark[bk] = ("semV", mk)
                    last = mk
                    if spanm_out is not None and f == NCHUNK - 1:
                        spanm_out.append(("semV", mk))
                else:
                    ew = [("semT", nmm), ("dml", vec_loaded)] + list(evac_waits)
                    if bias_col is not None:
                        bap = VEC[:, bias_col + f:bias_col + f + 1]
                        B.emit("S", lambda sc, f=f, s=s, ps=ps, bap=bap:
                               sc.activation(
                            dst[:, f, s * SPW:(s + 1) * SPW], ps,
                            act or AF.Identity, bias=bap),
                            inc="semS", waits=ew)
                    else:
                        B.emit("S", lambda sc, f=f, s=s, ps=ps: sc.activation(
                            dst[:, f, s * SPW:(s + 1) * SPW], ps,
                            act or AF.Identity),
                            inc="semS", waits=ew)
                    mk = B.n("semS")
                    bankmark[bk] = ("semS", mk)
                    last = mk
                    if spanm_out is not None and f == NCHUNK - 1:
                        spanm_out.append(("semS", mk))
            tmarks[f] = B.n("semT")
        return last

    def v_projection(WVF, w_hbm, src, src_ready, use_bias, VO, grp,
                     dma_guard=False, kv_guard=None, no_sync=False,
                     first_waits=()):
        if not no_sync:
            phase_sync()
        w_r = w_hbm.rearrange("(c p) f -> p c f", p=128)
        B.emit("V", lambda v: v.memset(VO[:, :, :, 64:128], 0.0),
            inc="semV", waits=[("semT", B.n("semT"))])
        B.emit("V", lambda v: v.memset(VO[:, :, :, 64:65], 1.0), inc="semV")
        hmark = ("semT", B.n("semT")) if grp == 1 else None
        for qq in range(2):
            qr = grp * 2 + qq
            dw = [hmark]
            if qq == 0 and dma_guard:
                dw += now_sv()
            B.emit("G", dma(WVF[:, :, :],
                            w_r[:, :, qr * 256:(qr + 1) * 256]),
                   inc="dml", k=16, waits=dw)
            nw = B.n("dml")
            vmarks = {}
            for tt in range(16):
                ps = (PSB[:, 0:512], PSB[:, 512:1024],
                      PSB[:, 1024:1536], PSC[:, :])[tt % 4]
                wts = [("dml", nw), ("semV", vmarks.get(tt - 4)),
                       ("semV", src_ready)]
                if tt % 4 == 3:
                    wts.append(("semV", kv_guard))
                if qq == 0 and tt == 0:
                    wts = wts + list(first_waits)
                for c in range(NCHUNK):
                    B.emit("T", lambda t, tt=tt, c=c, ps=ps: t.matmul(
                        ps[:, 0:256], src[:, c, tt * 128:(tt + 1) * 128],
                        WVF[:, c, :],
                        start=(c == 0),
                        stop=(c == NCHUNK - 1 and not use_bias)),
                        inc="semT", waits=wts if c == 0 else ())
                if use_bias:
                    B.emit("T", lambda t, qr=qr, ps=ps: t.matmul(
                        ps[:, 0:256], ONE1[0:1, :],
                        BVROW[0:1, qr * 256:(qr + 1) * 256],
                        start=False, stop=True),
                        inc="semT", waits=[("semV", consts_v)])
                nmm = B.n("semT")
                B.emit("V", lambda v, tt=tt, qq=qq, ps=ps: v.tensor_copy(
                    VO[:, tt, qq * 4:(qq + 1) * 4, 0:64],
                    ps[:, 0:256].rearrange("p (h d) -> p h d", h=4)),
                    inc="semV", waits=[("semT", nmm)])
                vmarks[tt] = B.n("semV")
            hmark = ("semT", B.n("semT"))
        return B.n("semV")

    def rope_apply(buf, ready_s):
        vring = {0: None, 1: None}
        for c in range(NCHUNK):
            for half in range(2):
                for q in range(2):
                    pssw = (PSA[:, 0:512] if q == 0 else PSA[:, 512:1024])
                    sl = slice(half * 1024 + q * 512,
                               half * 1024 + (q + 1) * 512)
                    B.emit("T", lambda t, c=c, sl=sl, pssw=pssw: t.matmul(
                        pssw, PSW[:, :], buf[:, c, sl],
                        start=True, stop=True),
                        inc="semT",
                        waits=[("semS", ready_s), vring[q]])
                    nsw = B.n("semT")
                    B.emit("V", lambda v, c=c, sl=sl: v.tensor_tensor(
                        ATMP[:, sl], buf[:, c, sl], COSb[:, sl],
                        op=ALU.mult),
                        inc="semV", waits=[("semS", ready_s)])
                    B.emit("V", lambda v, c=c, sl=sl, pssw=pssw:
                           v.tensor_tensor(
                        buf[:, c, sl], pssw, SINSb[:, sl],
                        op=ALU.mult),
                        inc="semV", waits=[("semT", nsw)])
                    B.emit("V", lambda v, c=c, sl=sl: v.tensor_tensor(
                        buf[:, c, sl], buf[:, c, sl], ATMP[:, sl],
                        op=ALU.add), inc="semV")
                    vring[q] = ("semV", B.n("semV"))
        return B.n("semV")

    def attention_core(st, l0, QT, KT, VO, grp, v_ready, qk_ready_v,
                       qk_ready_s, TANH=None):
        phase_sync()
        expm = st.setdefault("expm", {})
        pvtm = st.setdefault("pvtm", {})
        pv_last = st.get("pv_last")
        drow_dma = st.setdefault("drow_dma", {0: None, 1: None})

        def emit_st(pair, s, kc):
            bank = kc % 3
            for hb in (0, 1):
                base = 64 * hb
                psst = (PSA if hb == 0 else PSB)[:, bank * 512:(bank + 1) * 512]
                B.emit("T", lambda t, pair=pair, s=s, kc=kc, base=base,
                       psst=psst: t.matmul(
                    psst,
                    KT[base:base + 64, pair, kc * 128:(kc + 1) * 128],
                    QT[base:base + 64, pair, s * SPW:(s + 1) * SPW],
                    start=True, stop=True, tile_position=(base, 0)),
                    inc="semT",
                    waits=[expm.get((hb, bank)), ("semV", qk_ready_v),
                           ("semS", qk_ready_s), ("semV", v_ready)])
                st.setdefault("stm", {})[(hb, kc)] = B.n("semT")

        for pair in range(grp * 4, grp * 4 + 4):
            for s in range(NSPAN):
                nkc = 4 * s + 4
                emit_st(pair, s, 0)
                emit_st(pair, s, 1)
                for kc0 in range(0, nkc, 2):
                    if kc0 + 2 < nkc:
                        emit_st(pair, s, kc0 + 2)
                    b0, b1 = kc0 % 3, (kc0 + 1) % 3
                    s0, s1 = kc0 % 2, (kc0 + 1) % 2
                    band = (kc0 + 1) - 4 * s >= 0
                    for hb in (0, 1):
                        pst = PSA if hb == 0 else PSB
                        ew = [("semT", st["stm"][(hb, kc0 + 1)]),
                              ("semT", pvtm.get((hb, s0))),
                              ("semT", pvtm.get((hb, s1)))]
                        if not band:
                            pin = bass.AP(pst.ap().tensor, b0 * 512,
                                          [[1536, 128],
                                           [(b1 - b0) * 512, 2], [1, 512]])
                            pout = bass.AP(PT.ap().tensor,
                                           hb * 1024 + s0 * 512,
                                           [[2048, 128],
                                            [(s1 - s0) * 512, 2], [1, 512]])
                            if l0:
                                B.emit("S", lambda sc, pin=pin, pout=pout:
                                       sc.activation(
                                    pout, pin, AF.Exp,
                                    scale=QSCALE0, bias=CB[:, 1:2]),
                                    inc="semS", waits=ew)
                            else:
                                B.emit("S", lambda sc, pin=pin:
                                       sc.activation(
                                    TANH[:, 0:1024], pin, AF.Tanh,
                                    scale=TANH_SC), inc="semS", waits=ew)
                                B.emit("S", lambda sc, pout=pout:
                                       sc.activation(
                                    pout, TANH[:, 0:1024], AF.Exp,
                                    scale=50.0, bias=CB[:, 2:3]),
                                    inc="semS")
                            mk = ("semS", B.n("semS"))
                            expm[(hb, b0)] = mk
                            expm[(hb, b1)] = mk
                        else:
                            for kc, bank, slot in ((kc0, b0, s0),
                                                   (kc0 + 1, b1, s1)):
                                j = kc - 4 * s
                                off = 128 * j if j > 0 else 0
                                psst = pst[:, bank * 512:(bank + 1) * 512]
                                if l0:
                                    B.emit("S", lambda sc, psst=psst, hb=hb,
                                           slot=slot, off=off: sc.activation(
                                        PT[:, hb, slot, off:512],
                                        psst[:, off:512], AF.Exp,
                                        scale=QSCALE0, bias=CB[:, 1:2]),
                                        inc="semS", waits=ew)
                                else:
                                    B.emit("S", lambda sc, psst=psst, off=off:
                                           sc.activation(
                                        TANH[:, off:512], psst[:, off:512],
                                        AF.Tanh, scale=TANH_SC),
                                        inc="semS", waits=ew)
                                    B.emit("S", lambda sc, hb=hb, slot=slot,
                                           off=off: sc.activation(
                                        PT[:, hb, slot, off:512],
                                        TANH[:, off:512], AF.Exp,
                                        scale=50.0, bias=CB[:, 2:3]),
                                        inc="semS")
                                expm[(hb, bank)] = ("semS", B.n("semS"))
                        if hb == 1 and kc0 + 3 < nkc:
                            emit_st(pair, s, kc0 + 3)
                        for kc, bank, slot in ((kc0, b0, s0),
                                               (kc0 + 1, b1, s1)):
                            j = kc - 4 * s
                            off = 128 * j if j > 0 else 0
                            nmask = None
                            if j >= 0:
                                if off > 0:
                                    B.emit("V", lambda v, hb=hb, slot=slot,
                                           off=off: v.memset(
                                        PT[:, hb, slot, 0:off], 0.0),
                                        inc="semV",
                                        waits=[expm[(hb, bank)],
                                               ("semT", pvtm.get((hb, slot)))])
                                B.emit("V", lambda v, hb=hb, slot=slot,
                                       off=off: v.tensor_tensor(
                                    PT[:, hb, slot, off:off + 128],
                                    PT[:, hb, slot, off:off + 128],
                                    MSK[:, 512:640], op=ALU.mult),
                                    inc="semV", waits=[expm[(hb, bank)]])
                                nmask = B.n("semV")
                            pvp = PSC if hb == 0 else PSD
                            lh = pair * 2 + hb - grp * 8
                            wpv = [expm[(hb, bank)]]
                            if nmask:
                                wpv.append(("semV", nmask))
                            if kc == 0 and pv_last is not None:
                                wpv.append(("semV", pv_last))
                            B.emit("T", lambda t, kc=kc, lh=lh, s=s, pvp=pvp,
                                   hb=hb, slot=slot: t.matmul(
                                pvp[:, :], VO[:, kc, lh, 0:128],
                                PT[:, hb, slot, :],
                                start=(kc == 0), stop=(kc == 4 * s + 3)),
                                inc="semT", waits=wpv)
                            pvtm[(hb, slot)] = B.n("semT")
                sl = slice(s * SPW, (s + 1) * SPW)
                par = (pair * NSPAN + s) % 2
                B.emit("V", lambda v, pair=pair, sl=sl: v.tensor_copy(
                    QT[0:64, pair, sl], PSC[0:64, :]),
                    inc="semV", waits=[("semT", pvtm[(0, (4 * s + 3) % 2)]),
                                       ("dml", drow_dma.get(par))])
                B.emit("V", lambda v, par=par: v.tensor_copy(
                    DROW[0:1, par, :], PSC[64:65, :]), inc="semV")
                B.emit("V", lambda v, pair=pair, sl=sl: v.tensor_copy(
                    QT[64:128, pair, sl], PSD[0:64, :]),
                    inc="semV", waits=[("semT", pvtm[(1, (4 * s + 3) % 2)])])
                B.emit("V", lambda v, par=par: v.tensor_copy(
                    DROW[32:33, par, :], PSD[64:65, :]), inc="semV")
                pv_last = B.n("semV")
                B.emit("G", lambda g, pair=pair, sl=sl, par=par: g.dma_start(
                    out=denh[2 * pair:2 * pair + 1, sl],
                    in_=DROW[0:1, par, :]),
                    inc="dml", k=16, waits=[("semV", pv_last)])
                B.emit("G", lambda g, pair=pair, sl=sl, par=par: g.dma_start(
                    out=denh[2 * pair + 1:2 * pair + 2, sl],
                    in_=DROW[32:33, par, :]),
                    inc="dml", k=16)
                drow_dma[par] = B.n("dml")
        st["pv_last"] = pv_last
        return B.n("semV")

    def attention_finish(st, QT):
        pv_last = st["pv_last"]
        B.emit("G", lambda g: g.dma_start(out=STAT[0:16, :], in_=denh),
               inc="dml", k=16, waits=[("dml", B.n("dml"))])
        nrb = B.n("dml")
        B.emit("S", lambda sc: sc.activation(STAT[0:16, :], STAT[0:16, :],
                                             AF.Ln),
               inc="semS", waits=[("dml", nrb)])
        B.emit("S", lambda sc: sc.activation(RECB[0:16, :], STAT[0:16, :],
                                             AF.Exp, scale=-1.0), inc="semS")
        nrec = B.n("semS")
        emk = {0: None, 1: None}
        for pair in range(NCHUNK):
            for s in range(NSPAN):
                sl = slice(s * SPW, (s + 1) * SPW)
                par = (pair * NSPAN + s) % 2
                ps = PSC if par == 0 else PSD
                B.emit("T", lambda t, pair=pair, sl=sl, ps=ps: t.matmul(
                    ps[:, :], PATS[:, pair, :], RECB[0:16, sl],
                    start=True, stop=True),
                    inc="semT",
                    waits=[("semS", nrec), ("semV", pv_last), emk[par]])
                nb2 = B.n("semT")
                B.emit("S", lambda sc, sl=sl, ps=ps, par=par: sc.activation(
                    ATMP[:, par * SPW:(par + 1) * SPW], ps[:, :],
                    AF.Identity),
                    inc="semS", waits=[("semT", nb2), emk[par]])
                nbe = B.n("semS")
                B.emit("V", lambda v, pair=pair, sl=sl, par=par:
                       v.tensor_tensor(
                    QT[:, pair, sl], QT[:, pair, sl],
                    ATMP[:, par * SPW:(par + 1) * SPW], op=ALU.mult),
                    inc="semV", waits=[("semS", nbe)])
                emk[par] = ("semV", B.n("semV"))
        return B.n("semV")

    init_es.close()
    def _phases():
        ln1_marks = stream_ln(X, XT_)
        ln1_done = ln1_marks[-1]
        ck('ln1', XT_)

        l0_es = ExitStack()
        open_scopes.append(l0_es)
        A0 = l0_es.enter_context
        QT = A0(nc.sbuf_tensor("QT0", [128, NCHUNK, T], BF16))
        KT = A0(nc.sbuf_tensor("KT0", [128, NCHUNK, T], BF16))
        VO = A0(nc.sbuf_tensor("VO0", [128, 16, 8, 128], BF16))
        WVF = A0(nc.sbuf_tensor("WVF0", [128, NCHUNK, 256], BF16))

        sQ = projection(wq0, XT_, "q0", moving_ready=ln1_marks, bias_col=0, dst=QT,
                        dma_guard=True)
        ck('q_raw', QT)
        q_roped = rope_apply(QT, sQ)
        ck('q_roped', QT)
        sK = projection(wk0, XT_, "k0", bias_col=8, dst=KT, no_sync=True,
                        mm_first_waits=[("semS", sQ)])
        k_roped = rope_apply(KT, sK)
        ast0 = {}
        v_done = v_projection(WVF, wv0, XT_, ln1_done, True, VO, 0,
                              no_sync=True, kv_guard=k_roped,
                              first_waits=[("semS", sK)])
        attention_core(ast0, True, QT, KT, VO, 0, v_done,
                       max(q_roped, k_roped), None)
        v_done = v_projection(WVF, wv0, XT_, ln1_done, True, VO, 1)
        attention_core(ast0, True, QT, KT, VO, 1, v_done,
                       max(q_roped, k_roped), None)
        ck('attn_raw', QT)
        attn0_done = attention_finish(ast0, QT)
        ck('attn0', QT)
        Y = XT_
        wo0m = []
        projection(wo0, QT, "o0", moving_ready=attn0_done, bias_col=24, dst=Y,
                   spanm_out=wo0m)
        ck('wo0', Y)
        ln2span = inner_ln_residual(Y, src_marks=wo0m)
        ck('res1', X)
        B.flush()
        l0_es.close()
        open_scopes.remove(l0_es)
        B.flush()
        trig_es.close()
        open_scopes.remove(trig_es)

        ln3_marks = stream_ln(X, XT_, skip_sum=True,
                              src_marks=[("semV", m) for m in ln2span])
        ln3_done = ln3_marks[-1]
        f0_es = ExitStack()
        open_scopes.append(f0_es)
        A1 = f0_es.enter_context
        HSP = A1(nc.sbuf_tensor("HSP", [128, 24, 1024], BF16))
        WGS = A1(nc.sbuf_tensor("WGS", [128, 2, NCHUNK, 128], BF16))
        WUS = A1(nc.sbuf_tensor("WUS", [128, 2, NCHUNK, 128], BF16))
        WDS = A1(nc.sbuf_tensor("WDS", [128, 2, 24, 128], BF16))
        GTMP = A1(nc.sbuf_tensor("GTMP", [128, 4, 512], BF16))
        wg_r = wg_h.rearrange("(c p) f -> p c f", p=128)
        wu_r = wu_h.rearrange("(c p) f -> p c f", p=128)
        wd_r = wd_h.rearrange("(c p) f -> p c f", p=128)

        bank_map = [(PSA, 0), (PSA, 1), (PSA, 2), (PSB, 0),
                    (PSB, 1), (PSB, 2), (PSC, None), (PSD, None)]

        def bank(i):
            tns, off = bank_map[i % 8]
            if off is None:
                return tns[:, :]
            return tns[:, off * 512:(off + 1) * 512]

        phase_sync()
        guard0 = now_sv()
        ffn0m = {}
        for S_ in range(2):
            hmarks = {}
            first_S = True
            for hM in range(24):
                dw = [("semT", hmarks.get(hM - 2))]
                if S_ == 0 and hM == 0:
                    dw += guard0
                B.emit("G", dma(WGS[:, hM % 2, :, :],
                                wg_r[:, :, hM * 128:(hM + 1) * 128]),
                       inc="dml", k=16, waits=dw)
                B.emit("G", dma(WUS[:, hM % 2, :, :],
                                wu_r[:, :, hM * 128:(hM + 1) * 128]), inc="dml", k=16)
                nw = B.n("dml")
                for ss in range(2):
                    tok0 = S_ * 1024 + ss * 512
                    gbank = bank(ss * 2 + (hM % 2))
                    ubank = bank(4 + ss * 2 + (hM % 2))
                    wts = [("dml", nw), ("semV", ln3_done),
                           ("semV", hmarks.get((hM - 2, ss))),
                           ("semS", hmarks.get((hM - 2, ss, "s")))]
                    if first_S:
                        wts += now_sv()
                        first_S = False
                    for c in range(NCHUNK):
                        B.emit("T", lambda t, c=c, hM=hM, tok0=tok0, gbank=gbank:
                               t.matmul(
                            gbank, WGS[:, hM % 2, c, :],
                            XT_[:, c, tok0:tok0 + 512],
                            start=(c == 0), stop=(c == NCHUNK - 1)),
                            inc="semT", waits=wts if c == 0 else ())
                    for c in range(NCHUNK):
                        B.emit("T", lambda t, c=c, hM=hM, tok0=tok0, ubank=ubank:
                               t.matmul(
                            ubank, WUS[:, hM % 2, c, :],
                            XT_[:, c, tok0:tok0 + 512],
                            start=(c == 0), stop=(c == NCHUNK - 1)), inc="semT")
                    nmm = B.n("semT")
                    gt = GTMP[:, ss * 2 + hM % 2, :]
                    B.emit("S", lambda sc, gt=gt, gbank=gbank: sc.activation(
                        gt, gbank, AF.Silu, scale=DEQ),
                        inc="semS",
                        waits=[("semT", nmm), ("semV", hmarks.get((hM - 2, ss)))])
                    nsl = B.n("semS")
                    hmarks[(hM, ss, "s")] = nsl
                    B.emit("V", lambda v, hM=hM, ss=ss, gt=gt, ubank=ubank:
                           v.tensor_tensor(
                        HSP[:, hM, ss * 512:(ss + 1) * 512], gt, ubank,
                        op=ALU.mult), inc="semV", waits=[("semS", nsl)])
                    hmarks[(hM, ss)] = B.n("semV")
                hmarks[hM] = B.n("semT")
            h_done = B.n("semV")
            dmarks = {}
            for d in range(NCHUNK):
                B.emit("G", dma(WDS[:, d % 2, :, :],
                                wd_r[:, :, d * 128:(d + 1) * 128]),
                       inc="dml", k=16, waits=[("semT", dmarks.get(d - 2))])
                nw = B.n("dml")
                for ss in range(2):
                    tok0 = S_ * 1024 + ss * 512
                    dbank = bank(ss * 2 + (d % 2))
                    wts = [("dml", nw), ("semV", h_done),
                           ("semS", dmarks.get((d - 2, ss)))]
                    for hM in range(24):
                        B.emit("T", lambda t, hM=hM, d=d, ss=ss, dbank=dbank:
                               t.matmul(
                            dbank, WDS[:, d % 2, hM, :],
                            HSP[:, hM, ss * 512:(ss + 1) * 512],
                            start=(hM == 0), stop=(hM == 23)),
                            inc="semT", waits=wts if hM == 0 else ())
                    nmm = B.n("semT")
                    B.emit("S", lambda sc, d=d, tok0=tok0, dbank=dbank:
                           sc.activation(
                        XT_[:, d, tok0:tok0 + 512], dbank, AF.Identity,
                        scale=DEQ * DEQ), inc="semS", waits=[("semT", nmm)])
                    dmarks[(d, ss)] = B.n("semS")
                    if d == NCHUNK - 1:
                        ffn0m[S_ * 2 + ss] = ("semS", dmarks[(d, ss)])
                dmarks[d] = B.n("semT")
        ck('ffn0', XT_)
        B.flush()
        f0_es.close()
        open_scopes.remove(f0_es)
        ln4span = inner_ln_residual(XT_,
                                    src_marks=[ffn0m[3]] * 4)
        ln4_done = ln4span[-1]
        ck('res2', X)

        ln5_marks = stream_ln(X, XT_, skip_sum=True,
                              src_marks=[("semV", m) for m in ln4span])
        ln5_done = ln5_marks[-1]
        l1_es = ExitStack()
        open_scopes.append(l1_es)
        A2 = l1_es.enter_context
        QT = A2(nc.sbuf_tensor("QT1", [128, NCHUNK, T], BF16))
        KT = A2(nc.sbuf_tensor("KT1", [128, NCHUNK, T], BF16))
        VO = A2(nc.sbuf_tensor("VO1", [128, 16, 8, 128], BF16))
        WVF = A2(nc.sbuf_tensor("WVF1", [128, NCHUNK, 256], BF16))
        TANH = A2(nc.sbuf_tensor("TANH", [128, 1024], F32))

        sQ1 = projection(wq1, XT_, "q1", moving_ready=ln5_marks, dst=QT,
                         dma_guard=True)
        sK1 = projection(wk1, XT_, "k1", dst=KT, no_sync=True,
                         mm_first_waits=[("semS", sQ1)])
        ast1 = {}
        v1_done = v_projection(WVF, wv1, XT_, ln5_done, False, VO, 0,
                              no_sync=True, first_waits=[("semS", sK1)])
        attention_core(ast1, False, QT, KT, VO, 0, v1_done, None,
                       max(sQ1, sK1), TANH=TANH)
        v1_done = v_projection(WVF, wv1, XT_, ln5_done, False, VO, 1)
        attention_core(ast1, False, QT, KT, VO, 1, v1_done, None,
                       max(sQ1, sK1), TANH=TANH)
        attn1_done = attention_finish(ast1, QT)
        Y = XT_
        wo1m = []
        projection(wo1, QT, "o1", moving_ready=attn1_done, dst=Y,
                   spanm_out=wo1m)
        ck('wo1', Y)
        ln6span = inner_ln_residual(Y, src_marks=wo1m)
        ck('res3', X)
        B.flush()
        l1_es.close()
        open_scopes.remove(l1_es)

        ln7_marks = stream_ln(X, XT_, skip_sum=True,
                              src_marks=[("semV", m) for m in ln6span])
        ln7_done = ln7_marks[-1]
        f1_es = ExitStack()
        open_scopes.append(f1_es)
        A3 = f1_es.enter_context
        BUF1 = A3(nc.sbuf_tensor("BUF1", [128, NCHUNK, T], BF16))
        RR = A3(nc.sbuf_tensor("RR", [128, NCHUNK, T], BF16))
        KREL = A3(nc.sbuf_tensor("KREL", [128, NCHUNK, T], BF16))
        XP = A3(nc.sbuf_tensor("XP", [128, T], BF16))

        def token_shift(mu_col, dst, rdy_marks):
            marks = []
            for s in range(NSPAN):
                sl = slice(s * SPW, (s + 1) * SPW)
                for c in range(NCHUNK):
                    if s == 0:
                        B.emit("V", lambda v: v.memset(XP[:, 0:1], 0.0),
                               inc="semV", waits=[("semV", rdy_marks[0])])
                        B.emit("V", lambda v, c=c: v.tensor_copy(
                            XP[:, 1:SPW], XT_[:, c, 0:SPW - 1]), inc="semV")
                    else:
                        B.emit("V", lambda v, c=c, s=s: v.tensor_copy(
                            XP[:, 0:SPW],
                            XT_[:, c, s * SPW - 1:(s + 1) * SPW - 1]),
                            inc="semV", waits=[("semV", rdy_marks[s])])
                    B.emit("V", lambda v, c=c, sl=sl: v.tensor_tensor(
                        ATMP[:, sl], XT_[:, c, sl], XP[:, 0:SPW],
                        op=ALU.subtract), inc="semV")
                    B.emit("V", lambda v, c=c, sl=sl, mu_col=mu_col:
                           v.scalar_tensor_tensor(
                        dst[:, c, sl], ATMP[:, sl],
                        VEC[:, mu_col + c:mu_col + c + 1],
                        XP[:, 0:SPW], op0=ALU.mult, op1=ALU.add), inc="semV")
                marks.append(B.n("semV"))
            return marks

        xrm = token_shift(32, BUF1, ln7_marks)
        xkm = token_shift(40, KREL, ln7_marks)
        sR = projection(wr1, BUF1, "r1", moving_ready=xrm, act=AF.Sigmoid,
                        dst=RR, dma_guard=True)
        sKk = projection(wk1f, KREL, "kk1", moving_ready=xkm, act=AF.Relu,
                         dst=BUF1, no_sync=True,
                         mm_first_waits=[("semS", sR)])
        for c in range(NCHUNK):
            B.emit("V", lambda v, c=c: v.tensor_tensor(
                BUF1[:, c, :], BUF1[:, c, :], BUF1[:, c, :], op=ALU.mult),
                inc="semV", waits=[("semS", sKk)])
        ka_done = B.n("semV")

        def kv_evac(f, s, ps, nmm):
            B.emit("V", lambda v, f=f, s=s, ps=ps: v.tensor_tensor(
                XT_[:, f, s * SPW:(s + 1) * SPW], ps,
                RR[:, f, s * SPW:(s + 1) * SPW], op=ALU.mult),
                inc="semV", waits=[("semT", nmm), ("semS", sR)])
            return B.n("semV")

        kv1m = []
        projection(wv1f, BUF1, "kv1", evac=kv_evac, moving_ready=ka_done,
                   no_sync=True, mm_first_waits=[("semS", sKk)],
                   spanm_out=kv1m)
        ck('ffn1', XT_)
        B.flush()
        f1_es.close()
        open_scopes.remove(f1_es)
        ln8_done = inner_ln_residual(XT_, src_marks=kv1m)[-1]


    try:
        _phases()
    except _StopEmit:
        pass
    B.flush()
    for sc_ in reversed(open_scopes):
        sc_.close()
    if stop_state:
        out_src, mv, ms = stop_state[0]
    else:
        out_src, mv, ms = X, B.n("semV"), B.n("semS")
    B.emit("G", lambda g: g.dma_start(
        out=out_h.rearrange("(c p) t -> p c t", p=128), in_=out_src[:, :, :]),
        inc="dms", k=16, waits=[("semV", mv), ("semS", ms)])
    B.emit("G", lambda g: g.nop(), waits=[("dms", B.n("dms"))])
    B.flush()

    blk.__exit__(None, None, None)
    es.close()
    return nc


def _shard_inputs(inputs):
    x = np.ascontiguousarray(np.asarray(inputs["L0_attn_x"], np.float32))
    vecs = np.zeros((128, 48), dtype=np.float32)
    for i, nm in enumerate(["L0_attn_bq", "L0_attn_bk", "L0_attn_bv",
                            "L0_attn_bo"]):
        vecs[:, i * 8:(i + 1) * 8] = np.asarray(
            inputs[nm], np.float32).reshape(8, 128).T
    vecs[:, 32:40] = np.asarray(inputs["L1_ffn_mu_r"],
                                np.float32).reshape(8, 128).T
    vecs[:, 40:48] = np.asarray(inputs["L1_ffn_mu_k"],
                                np.float32).reshape(8, 128).T
    rows4 = np.asarray(inputs["L0_attn_bv"], np.float32).reshape(1, D)

    def tr(a, dt=np.float32):
        return np.ascontiguousarray(np.asarray(a, dt).T)

    shared = {
        "wq0": tr(inputs["L0_attn_Wq"]), "wk0": tr(inputs["L0_attn_Wk"]),
        "wv0": tr(inputs["L0_attn_Wv"]), "wo0": tr(inputs["L0_attn_Wo"]),
        "vecs": vecs, "rows4": np.ascontiguousarray(rows4),
        "wg": tr(inputs["L0_ffn_w_gate"], np.int32),
        "wu": tr(inputs["L0_ffn_w_up"], np.int32),
        "wd": tr(inputs["L0_ffn_w_down"], np.int32),
        "wq1": tr(inputs["L1_attn_Wq"]), "wk1": tr(inputs["L1_attn_Wk"]),
        "wv1": tr(inputs["L1_attn_Wv"]), "wo1": tr(inputs["L1_attn_Wo"]),
        "wr1": tr(inputs["L1_ffn_W_r"]), "wk1f": tr(inputs["L1_ffn_W_k"]),
        "wv1f": tr(inputs["L1_ffn_W_v"]),
    }
    pats = np.zeros((16, 1024), np.float32)
    for p in range(8):
        pats[2 * p, p * 128:p * 128 + 64] = 1
        pats[2 * p + 1, p * 128 + 64:p * 128 + 128] = 1
    msk = (np.arange(1024)[None, :] >= np.arange(128)[:, None] + 512)
    psw = np.zeros((128, 128), np.float32)
    psw[np.arange(128), np.arange(128) ^ 32] = 1
    cpos = np.broadcast_to(np.arange(1, 2049, dtype=np.float32)[None, :],
                           (128, 2048))
    dmod = (np.arange(128) & 31).astype(np.float32)
    sgn = np.where((np.arange(128) & 32) != 0, 1.0, -1.0).astype(np.float32)
    shared["cpats"] = pats
    shared["cmsk"] = np.ascontiguousarray(msk.astype(np.float32))
    shared["cpsw"] = psw
    shared["cpos"] = np.ascontiguousarray(cpos)
    shared["cdd"] = np.ascontiguousarray(np.stack([dmod, sgn], axis=1))
    in_maps = []
    for b in range(8):
        m = dict(shared)
        m["xT"] = np.ascontiguousarray(x[b].T)
        in_maps.append(m)
    return in_maps


def kernel(**inputs):
    in_maps = _shard_inputs(inputs)
    nc = build_nc()
    res = run_bass_kernel_spmd(nc, in_maps, core_ids=list(range(8)))
    out = np.stack([np.ascontiguousarray(r["out"].T) for r in res.results])
    return out


# version 22
# speedup vs baseline: 1.0065x; 1.0065x over previous
import numpy as np
from contextlib import ExitStack

import concourse.bass as bass
import concourse.mybir as mybir
from concourse.bass_utils import run_bass_kernel_spmd

F32 = mybir.dt.float32
BF16 = mybir.dt.bfloat16
I32 = mybir.dt.int32
AF = mybir.ActivationFunctionType
ALU = mybir.AluOpType

T = 2048
D = 1024
NCHUNK = 8
NSPAN = 4
SPW = 512
EPS = 1e-5
C0 = 4.0
C1 = 12.0
QSCALE0 = 0.125
TANH_SC = 0.125 / 50.0
DEQ = 1.0 / 127.0
R_SUM, R_SQ, R_VAR = 0, 32, 64


class Bld:
    def __init__(self, nc, block, es):
        self.nc = nc
        self.sec = {"T": block.tensor, "S": block.scalar, "V": block.vector,
                    "G": block.gpsimd, "Y": block.sync}
        self.sem = {}
        self.seen = {}
        self.buf = {"T": [], "S": [], "V": [], "G": [], "Y": []}
        self.owner = {"semT": "T", "semS": "S", "semV": "V", "semG": "G"}
        for name in ("semT", "semS", "semV", "semG", "dml", "dms"):
            h = es.enter_context(nc.semaphore(name))
            self.sem[name] = [h, 0]

    def n(self, name):
        return self.sem[name][1]

    def emit(self, eng, f, inc=None, k=1, waits=()):
        real = []
        for w in waits:
            if w is None:
                continue
            sname, val = w
            if val is None or val <= 0:
                continue
            if self.owner.get(sname) == eng:
                continue
            key = (eng, sname)
            if self.seen.get(key, 0) >= val:
                continue
            self.seen[key] = val
            real.append((self.sem[sname][0], val))
        incsem = None
        if inc:
            self.sem[inc][1] += k
            incsem = (self.sem[inc][0], k)

        def body(e):
            for h, v in real:
                e.wait_ge(h, v)
            ins = f(e)
            if incsem is not None:
                if isinstance(ins, (list, tuple)):
                    ins = ins[-1]
                ins.then_inc(*incsem)
        self.buf[eng].append(body)
        return self.sem[inc][1] if inc else None

    def flush(self):
        for eng, bodies in self.buf.items():
            if bodies:
                bs = list(bodies)
                bodies.clear()

                def runner(e, bs=bs):
                    for b in bs:
                        b(e)
                self.sec[eng](runner)


class _StopEmit(Exception):
    pass


def build_nc(tap=None, upto=None):
    nc = bass.Bass("TRN2", target_bir_lowering=False,
                   detect_race_conditions=False)

    xT = nc.dram_tensor("xT", [D, T], F32, kind="ExternalInput").ap()
    wq0 = nc.dram_tensor("wq0", [D, D], F32, kind="ExternalInput").ap()
    wk0 = nc.dram_tensor("wk0", [D, D], F32, kind="ExternalInput").ap()
    wv0 = nc.dram_tensor("wv0", [D, D], F32, kind="ExternalInput").ap()
    wo0 = nc.dram_tensor("wo0", [D, D], F32, kind="ExternalInput").ap()
    vecs_h = nc.dram_tensor("vecs", [128, 48], F32, kind="ExternalInput").ap()
    rows_h = nc.dram_tensor("rows4", [1, D], F32, kind="ExternalInput").ap()
    wg_h = nc.dram_tensor("wg", [D, 3072], I32, kind="ExternalInput").ap()
    wu_h = nc.dram_tensor("wu", [D, 3072], I32, kind="ExternalInput").ap()
    wd_h = nc.dram_tensor("wd", [3072, D], I32, kind="ExternalInput").ap()
    wq1 = nc.dram_tensor("wq1", [D, D], F32, kind="ExternalInput").ap()
    wk1 = nc.dram_tensor("wk1", [D, D], F32, kind="ExternalInput").ap()
    wv1 = nc.dram_tensor("wv1", [D, D], F32, kind="ExternalInput").ap()
    wo1 = nc.dram_tensor("wo1", [D, D], F32, kind="ExternalInput").ap()
    wr1 = nc.dram_tensor("wr1", [D, D], F32, kind="ExternalInput").ap()
    wk1f = nc.dram_tensor("wk1f", [D, D], F32, kind="ExternalInput").ap()
    wv1f = nc.dram_tensor("wv1f", [D, D], F32, kind="ExternalInput").ap()
    out_h = nc.dram_tensor("out", [D, T], F32, kind="ExternalOutput").ap()
    denh = nc.dram_tensor("denh", [16, T], BF16, kind="Internal").ap()
    cpats = nc.dram_tensor("cpats", [16, 1024], F32, kind="ExternalInput").ap()
    cmsk = nc.dram_tensor("cmsk", [128, 1024], F32, kind="ExternalInput").ap()
    cpsw = nc.dram_tensor("cpsw", [128, 128], F32, kind="ExternalInput").ap()
    cpos = nc.dram_tensor("cpos", [128, T], F32, kind="ExternalInput").ap()
    cdd = nc.dram_tensor("cdd", [128, 2], F32, kind="ExternalInput").ap()
    tap_h = None
    if tap is not None:
        tap_h = nc.dram_tensor("tap", [D, T], F32, kind="ExternalOutput").ap()

    es = ExitStack()
    A = es.enter_context

    X = A(nc.sbuf_tensor("X", [128, NCHUNK, T], BF16))
    XT_ = A(nc.sbuf_tensor("XT", [128, NCHUNK, T], BF16))
    STAT = A(nc.sbuf_tensor("STAT", [65, T], F32))
    RECB = A(nc.sbuf_tensor("RECB", [65, T], BF16))
    PT = A(nc.sbuf_tensor("PT", [128, 2, 2, SPW], BF16))
    MB = PT[:, 0, :, :]
    RB = PT[:, 1, :, :]
    VEC = A(nc.sbuf_tensor("VEC", [128, 48], F32))
    BVROW = A(nc.sbuf_tensor("BVROW", [1, D], BF16))
    ONE1 = A(nc.sbuf_tensor("ONE1", [65, 128], BF16))
    ONEC = A(nc.sbuf_tensor("ONEC", [128, 1], BF16))
    CB = A(nc.sbuf_tensor("CB", [128, 4], F32))
    PATS = A(nc.sbuf_tensor("PATS", [16, NCHUNK, 128], BF16))
    DROW = A(nc.sbuf_tensor("DROW", [33, 2, 512], BF16))
    MSK = A(nc.sbuf_tensor("MSK", [128, 1024], BF16))
    ATMP = A(nc.sbuf_tensor("ATMP", [128, T], BF16))
    WSL = A(nc.sbuf_tensor("WSL", [128, 2, NCHUNK, 128], BF16))
    X2S = A(nc.sbuf_tensor("X2S", [128, 2, SPW], BF16))

    PSA = A(nc.psum_tensor("PSA", [128, 1536], F32))
    PSB = A(nc.psum_tensor("PSB", [128, 1536], F32))
    PSC = A(nc.psum_tensor("PSC", [128, 512], F32))
    PSD = A(nc.psum_tensor("PSD", [128, 512], F32))

    blk = nc.Block()
    block = blk.__enter__()
    B = Bld(nc, block, es)

    def dma(out, in_):
        return lambda g: g.dma_start(out=out, in_=in_)

    def phase_sync():
        B.flush()
        B.emit("T", lambda t: t.nop(), inc="semT",
               waits=[("semS", B.n("semS")), ("semV", B.n("semV")),
                      ("semG", B.n("semG"))])

    def now_sv():
        return [("semS", B.n("semS")), ("semV", B.n("semV"))]

    stop_state = []
    open_scopes = []

    def ck(stage, buf):
        if upto == stage:
            stop_state.append((buf, B.n("semV"), B.n("semS")))
            raise _StopEmit()

    trig_es = ExitStack()
    open_scopes.append(trig_es)
    COSb = trig_es.enter_context(nc.sbuf_tensor("COSb", [128, T], BF16))
    SINSb = trig_es.enter_context(nc.sbuf_tensor("SINSb", [128, T], BF16))
    PSW = trig_es.enter_context(nc.sbuf_tensor("PSW", [128, 128], BF16))
    init_es = ExitStack()
    Ai = init_es.enter_context
    ROWS = Ai(nc.sbuf_tensor("ROWS", [1, D], F32))
    ITMP = Ai(nc.sbuf_tensor("ITMP", [128, T], I32))
    FTI = Ai(nc.sbuf_tensor("FTI", [128, T], F32))
    FTI2 = Ai(nc.sbuf_tensor("FTI2", [128, T], F32))
    FTIP = Ai(nc.sbuf_tensor("FTIP", [128, T], F32))
    DD = Ai(nc.sbuf_tensor("DD", [128, 6], F32))
    CDD = Ai(nc.sbuf_tensor("CDD", [128, 2], F32))

    B.emit("G", dma(X[:, :, :], xT.rearrange("(c p) t -> p c t", p=128)),
           inc="dml", k=16)
    x_loaded = B.n("dml")
    B.emit("G", dma(VEC[:, :], vecs_h), inc="dml", k=16)
    B.emit("G", dma(ROWS[:, :], rows_h), inc="dml", k=16)
    vec_loaded = B.n("dml")

    B.emit("G", dma(MSK[:, :], cmsk), inc="dml", k=16)
    B.emit("G", dma(PATS[:, :, :].rearrange("k a j -> k (a j)"), cpats),
           inc="dml", k=16)
    B.emit("G", dma(PSW[:, :], cpsw), inc="dml", k=16)
    B.emit("G", dma(FTIP[:, :], cpos), inc="dml", k=16)
    B.emit("G", dma(CDD[:, :], cdd), inc="dml", k=16)
    consts_dma = B.n("dml")
    B.emit("V", lambda v: v.memset(ONE1[0:1, :], 1.0), inc="semV")
    B.emit("V", lambda v: v.memset(ONE1[32:33, :], 1.0), inc="semV")
    B.emit("V", lambda v: v.memset(ONE1[64:65, :], 1.0), inc="semV")
    B.emit("V", lambda v: v.memset(ONEC[:, :], 1.0), inc="semV")
    B.emit("V", lambda v: v.memset(CB[:, 0:1], EPS), inc="semV")
    B.emit("V", lambda v: v.memset(CB[:, 1:2], -C0), inc="semV")
    B.emit("V", lambda v: v.memset(CB[:, 2:3], -C1), inc="semV")
    B.emit("V", lambda v: v.memset(RECB[:, :], 0.0), inc="semV")
    B.emit("V", lambda v: v.tensor_copy(BVROW[0:1, :], ROWS[0:1, :]),
           inc="semV", waits=[("dml", vec_loaded)])
    consts_v = B.n("semV")

    LN1E4 = float(np.log(10000.0))
    B.emit("S", lambda s: s.activation(DD[:, 2:3], CDD[:, 0:1], AF.Exp,
                                       scale=-LN1E4 / 32.0),
           inc="semS", waits=[("dml", consts_dma)])
    th_done = B.n("semS")
    INV2PI = float(1.0 / (2 * np.pi))
    TWOPI = float(2 * np.pi)

    def rope_trig(out_bf, quarter_shift, sign_mult):
        B.emit("V", lambda v: v.tensor_scalar(FTI[:, :], FTIP[:, :],
                                              DD[:, 2:3], None, op0=ALU.mult),
               inc="semV",
               waits=[("semS", th_done), ("dml", consts_dma)])
        B.emit("V", lambda v: v.tensor_scalar(FTI[:, :], FTI[:, :], INV2PI,
                                              quarter_shift, op0=ALU.mult,
                                              op1=ALU.add), inc="semV")
        B.emit("V", lambda v: v.tensor_copy(ITMP[:, :], FTI[:, :]),
               inc="semV")
        B.emit("V", lambda v: v.tensor_tensor(FTI[:, :], FTI[:, :],
                                              ITMP[:, :], op=ALU.subtract),
               inc="semV")
        B.emit("V", lambda v: v.tensor_scalar(FTI2[:, :], FTI[:, :], 0.5,
                                              None, op0=ALU.is_ge),
               inc="semV")
        B.emit("V", lambda v: v.tensor_tensor(FTI[:, :], FTI[:, :],
                                              FTI2[:, :], op=ALU.subtract),
               inc="semV")
        B.emit("V", lambda v: v.tensor_scalar(FTI2[:, :], FTI[:, :], -0.5,
                                              None, op0=ALU.is_lt),
               inc="semV")
        B.emit("V", lambda v: v.tensor_tensor(FTI[:, :], FTI[:, :],
                                              FTI2[:, :], op=ALU.add),
               inc="semV")
        B.emit("V", lambda v: v.tensor_scalar(FTI[:, :], FTI[:, :], TWOPI,
                                              None, op0=ALU.mult),
               inc="semV")
        n = B.n("semV")
        B.emit("S", lambda s: s.activation(FTI2[:, :], FTI[:, :], AF.Sin),
               inc="semS", waits=[("semV", n)])
        n2 = B.n("semS")
        if sign_mult:
            B.emit("V", lambda v: v.tensor_scalar(out_bf[:, :], FTI2[:, :],
                                                  CDD[:, 1:2], None,
                                                  op0=ALU.mult),
                   inc="semV", waits=[("semS", n2)])
        else:
            B.emit("V", lambda v: v.tensor_copy(out_bf[:, :], FTI2[:, :]),
                   inc="semV", waits=[("semS", n2)])

    rope_trig(COSb, 0.25, False)
    rope_trig(SINSb, 0.0, True)
    consts_done = B.n("semV")

    def ln_core(src, writeback, src_marks=None, sum_row=R_SUM,
                skip_sum=False):
        wv0 = B.n("semV")
        ws0 = B.n("semS")
        x2marks = {}
        evmarks = {}
        wbmark = {0: None, 1: None}
        evmark = {0: None, 1: None}
        span_marks = []
        for s in range(NSPAN):
            par = s % 2
            sm = src_marks[s] if src_marks is not None else ("semV", wv0)
            for c in range(NCHUNK):
                i = (s * NCHUNK + c) % 2
                B.emit("S", lambda sc, s=s, c=c, i=i: sc.activation(
                    X2S[:, i, :],
                    src[:, c, s * SPW:(s + 1) * SPW], AF.Square),
                    inc="semS",
                    waits=[("semT", x2marks.get(s * NCHUNK + c - 2)),
                           sm, ("dml", x_loaded), ("semS", ws0)])
                nsq = B.n("semS")
                if not skip_sum:
                    B.emit("T", lambda t, s=s, c=c, par=par: t.matmul(
                        PSA[0:1, par * SPW:(par + 1) * SPW], ONEC[:, 0:1],
                        src[:, c, s * SPW:(s + 1) * SPW],
                        start=(c == 0), stop=(c == NCHUNK - 1)),
                        inc="semT",
                        waits=([("semV", consts_done), sm,
                                ("semV", evmarks.get(s - 2))]
                               if c == 0 else ()))
                B.emit("T", lambda t, s=s, c=c, i=i, par=par: t.matmul(
                    PSB[0:1, par * SPW:(par + 1) * SPW], ONEC[:, 0:1],
                    X2S[:, i, :],
                    start=(c == 0), stop=(c == NCHUNK - 1)),
                    inc="semT", waits=[("semS", nsq)])
                x2marks[s * NCHUNK + c] = B.n("semT")
            nsp = B.n("semT")
            sl = slice(s * SPW, (s + 1) * SPW)
            if not skip_sum:
                B.emit("V", lambda v, sl=sl, par=par, sum_row=sum_row:
                       v.tensor_copy(
                    STAT[sum_row:sum_row + 1, sl],
                    PSA[0:1, par * SPW:(par + 1) * SPW]),
                    inc="semV", waits=[("semT", nsp)])
            B.emit("V", lambda v, sl=sl, sum_row=sum_row: v.tensor_tensor(
                STAT[R_VAR:R_VAR + 1, sl], STAT[sum_row:sum_row + 1, sl],
                STAT[sum_row:sum_row + 1, sl], op=ALU.mult),
                inc="semV", waits=[("semT", nsp)])
            B.emit("V", lambda v, sl=sl, par=par: v.scalar_tensor_tensor(
                STAT[R_VAR:R_VAR + 1, sl], STAT[R_VAR:R_VAR + 1, sl],
                -1.0 / D, PSB[0:1, par * SPW:(par + 1) * SPW],
                op0=ALU.mult, op1=ALU.add), inc="semV")
            evmarks[s] = B.n("semV")
            B.emit("V", lambda v, sl=sl, sum_row=sum_row: v.tensor_scalar(
                RECB[32:33, sl], STAT[sum_row:sum_row + 1, sl],
                1.0 / D, None, op0=ALU.mult), inc="semV")
            nrow = B.n("semV")
            B.emit("S", lambda sc, sl=sl: sc.activation(
                STAT[R_VAR:R_VAR + 1, sl], STAT[R_VAR:R_VAR + 1, sl], AF.Ln,
                scale=1.0 / D, bias=CB[64:65, 0:1]),
                inc="semS", waits=[("semV", nrow)])
            B.emit("S", lambda sc, sl=sl: sc.activation(
                RECB[64:65, sl], STAT[R_VAR:R_VAR + 1, sl], AF.Exp,
                scale=-0.5), inc="semS")
            nr = B.n("semS")
            psm = PSA[:, 1024:1536]
            psr = PSB[:, 1024:1536]
            B.emit("T", lambda t, sl=sl, psm=psm: t.matmul(
                psm, ONE1[32:33, :], RECB[32:33, sl],
                start=True, stop=True),
                inc="semT",
                waits=[("semV", nrow), ("semS", nr), evmark[0]])
            n1 = B.n("semT")
            B.emit("T", lambda t, sl=sl, psr=psr: t.matmul(
                psr, ONE1[64:65, :], RECB[64:65, sl],
                start=True, stop=True), inc="semT")
            n2 = B.n("semT")
            B.emit("S", lambda sc, par=par, psm=psm: sc.activation(
                MB[:, par, :], psm, AF.Identity),
                inc="semS", waits=[("semT", n1), wbmark[par]])
            B.emit("S", lambda sc, par=par, psr=psr: sc.activation(
                RB[:, par, :], psr, AF.Identity),
                inc="semS", waits=[("semT", n2)])
            nbc = B.n("semS")
            evmark[0] = ("semS", nbc)
            for c in range(NCHUNK):
                writeback(c, s, par, nbc)
            wbmark[par] = ("semV", B.n("semV"))
            span_marks.append(B.n("semV"))
        return span_marks

    def stream_ln(src, dst, src_marks=None, skip_sum=False):
        def wb(c, s, par, nbc):
            sl = slice(s * SPW, (s + 1) * SPW)
            B.emit("V", lambda v, c=c, sl=sl, par=par: v.tensor_tensor(
                ATMP[:, sl], src[:, c, sl], MB[:, par, :], op=ALU.subtract),
                inc="semV", waits=[("semS", nbc)])
            B.emit("V", lambda v, c=c, sl=sl, par=par: v.tensor_tensor(
                dst[:, c, sl], ATMP[:, sl], RB[:, par, :], op=ALU.mult),
                inc="semV")
        phase_sync()
        return ln_core(src, wb, src_marks, sum_row=R_SQ,
                       skip_sum=skip_sum)

    def inner_ln_residual(y, src_marks=None):
        def wb(c, s, par, nbc):
            sl = slice(s * SPW, (s + 1) * SPW)
            B.emit("V", lambda v, c=c, sl=sl, par=par: v.tensor_tensor(
                ATMP[:, sl], y[:, c, sl], MB[:, par, :], op=ALU.subtract),
                inc="semV", waits=[("semS", nbc)])
            B.emit("V", lambda v, c=c, sl=sl, par=par: v.tensor_tensor(
                ATMP[:, sl], ATMP[:, sl], RB[:, par, :], op=ALU.mult),
                inc="semV")
            B.emit("V", lambda v, c=c, sl=sl: v.tensor_tensor(
                X[:, c, sl], X[:, c, sl], ATMP[:, sl], op=ALU.add),
                inc="semV")
        phase_sync()
        return ln_core(y, wb, src_marks)

    def projection(w_hbm, src, name, evac=None, moving_ready=None,
                   bias_col=None, act=None, dst=None, evac_waits=(),
                   mm_first_waits=(), dma_guard=False, no_sync=False,
                   spanm_out=None):
        if not no_sync:
            phase_sync()
        w_r = w_hbm.rearrange("(c p) f -> p c f", p=128)
        tprev = B.n("semT")
        tmarks = {}
        bankmark = {0: None, 1: None}
        first = True
        last = None
        for f in range(NCHUNK):
            dw = [("semT", tmarks.get(f - 2, tprev))]
            if f == 0 and dma_guard:
                dw += now_sv()
            B.emit("G", dma(WSL[:, f % 2, :, :],
                            w_r[:, :, f * 128:(f + 1) * 128]),
                   inc="dml", k=16, waits=dw)
            ndma = B.n("dml")
            for s in range(NSPAN):
                bk = s % 2
                ps = PSC[:, :] if bk == 0 else PSD[:, :]
                wts = [("dml", ndma), bankmark[bk]]
                if moving_ready is not None:
                    mv = (moving_ready[s] if isinstance(moving_ready, list)
                          else moving_ready)
                    wts.append(("semV", mv))
                if first:
                    wts += list(mm_first_waits)
                    first = False
                for c in range(NCHUNK):
                    B.emit("T", lambda t, f=f, s=s, c=c, ps=ps: t.matmul(
                        ps, WSL[:, f % 2, c, :],
                        src[:, c, s * SPW:(s + 1) * SPW],
                        start=(c == 0), stop=(c == NCHUNK - 1)),
                        inc="semT", waits=wts if c == 0 else ())
                nmm = B.n("semT")
                if evac is not None:
                    mk = evac(f, s, ps, nmm)
                    bankmark[bk] = ("semV", mk)
                    last = mk
                    if spanm_out is not None and f == NCHUNK - 1:
                        spanm_out.append(("semV", mk))
                else:
                    ew = [("semT", nmm), ("dml", vec_loaded)] + list(evac_waits)
                    if bias_col is not None:
                        bap = VEC[:, bias_col + f:bias_col + f + 1]
                        B.emit("S", lambda sc, f=f, s=s, ps=ps, bap=bap:
                               sc.activation(
                            dst[:, f, s * SPW:(s + 1) * SPW], ps,
                            act or AF.Identity, bias=bap),
                            inc="semS", waits=ew)
                    else:
                        B.emit("S", lambda sc, f=f, s=s, ps=ps: sc.activation(
                            dst[:, f, s * SPW:(s + 1) * SPW], ps,
                            act or AF.Identity),
                            inc="semS", waits=ew)
                    mk = B.n("semS")
                    bankmark[bk] = ("semS", mk)
                    last = mk
                    if spanm_out is not None and f == NCHUNK - 1:
                        spanm_out.append(("semS", mk))
            tmarks[f] = B.n("semT")
        return last

    def v_projection(WVF, w_hbm, src, src_ready, use_bias, VO, grp,
                     dma_guard=False, kv_guard=None, no_sync=False,
                     first_waits=()):
        if not no_sync:
            phase_sync()
        w_r = w_hbm.rearrange("(c p) f -> p c f", p=128)
        B.emit("V", lambda v: v.memset(VO[:, :, :, 64:128], 0.0),
            inc="semV", waits=[("semT", B.n("semT"))])
        B.emit("V", lambda v: v.memset(VO[:, :, :, 64:65], 1.0), inc="semV")
        hmark = ("semT", B.n("semT")) if grp == 1 else None
        for qq in range(2):
            qr = grp * 2 + qq
            dw = [hmark]
            if qq == 0 and dma_guard:
                dw += now_sv()
            B.emit("G", dma(WVF[:, :, :],
                            w_r[:, :, qr * 256:(qr + 1) * 256]),
                   inc="dml", k=16, waits=dw)
            nw = B.n("dml")
            vmarks = {}
            for tt in range(16):
                ps = (PSB[:, 0:512], PSB[:, 512:1024],
                      PSB[:, 1024:1536], PSC[:, :])[tt % 4]
                wts = [("dml", nw), ("semV", vmarks.get(tt - 4)),
                       ("semV", src_ready)]
                if tt % 4 == 3:
                    wts.append(("semV", kv_guard))
                if qq == 0 and tt == 0:
                    wts = wts + list(first_waits)
                for c in range(NCHUNK):
                    B.emit("T", lambda t, tt=tt, c=c, ps=ps: t.matmul(
                        ps[:, 0:256], src[:, c, tt * 128:(tt + 1) * 128],
                        WVF[:, c, :],
                        start=(c == 0),
                        stop=(c == NCHUNK - 1 and not use_bias)),
                        inc="semT", waits=wts if c == 0 else ())
                if use_bias:
                    B.emit("T", lambda t, qr=qr, ps=ps: t.matmul(
                        ps[:, 0:256], ONE1[0:1, :],
                        BVROW[0:1, qr * 256:(qr + 1) * 256],
                        start=False, stop=True),
                        inc="semT", waits=[("semV", consts_v)])
                nmm = B.n("semT")
                B.emit("V", lambda v, tt=tt, qq=qq, ps=ps: v.tensor_copy(
                    VO[:, tt, qq * 4:(qq + 1) * 4, 0:64],
                    ps[:, 0:256].rearrange("p (h d) -> p h d", h=4)),
                    inc="semV", waits=[("semT", nmm)])
                vmarks[tt] = B.n("semV")
            hmark = ("semT", B.n("semT"))
        return B.n("semV")

    def rope_apply(buf, ready_s):
        vring = {0: None, 1: None}
        for c in range(NCHUNK):
            for half in range(2):
                for q in range(2):
                    pssw = (PSA[:, 0:512] if q == 0 else PSA[:, 512:1024])
                    sl = slice(half * 1024 + q * 512,
                               half * 1024 + (q + 1) * 512)
                    B.emit("T", lambda t, c=c, sl=sl, pssw=pssw: t.matmul(
                        pssw, PSW[:, :], buf[:, c, sl],
                        start=True, stop=True),
                        inc="semT",
                        waits=[("semS", ready_s), vring[q]])
                    nsw = B.n("semT")
                    B.emit("V", lambda v, c=c, sl=sl: v.tensor_tensor(
                        ATMP[:, sl], buf[:, c, sl], COSb[:, sl],
                        op=ALU.mult),
                        inc="semV", waits=[("semS", ready_s)])
                    B.emit("V", lambda v, c=c, sl=sl, pssw=pssw:
                           v.tensor_tensor(
                        buf[:, c, sl], pssw, SINSb[:, sl],
                        op=ALU.mult),
                        inc="semV", waits=[("semT", nsw)])
                    B.emit("V", lambda v, c=c, sl=sl: v.tensor_tensor(
                        buf[:, c, sl], buf[:, c, sl], ATMP[:, sl],
                        op=ALU.add), inc="semV")
                    vring[q] = ("semV", B.n("semV"))
        return B.n("semV")

    def attention_core(st, l0, QT, KT, VO, grp, v_ready, qk_ready_v,
                       qk_ready_s, TANH=None):
        phase_sync()
        expm = st.setdefault("expm", {})
        pvtm = st.setdefault("pvtm", {})
        pv_last = st.get("pv_last")
        drow_dma = st.setdefault("drow_dma", {0: None, 1: None})

        def emit_st(pair, s, kc):
            bank = kc % 3
            for hb in (0, 1):
                base = 64 * hb
                psst = (PSA if hb == 0 else PSB)[:, bank * 512:(bank + 1) * 512]
                B.emit("T", lambda t, pair=pair, s=s, kc=kc, base=base,
                       psst=psst: t.matmul(
                    psst,
                    KT[base:base + 64, pair, kc * 128:(kc + 1) * 128],
                    QT[base:base + 64, pair, s * SPW:(s + 1) * SPW],
                    start=True, stop=True, tile_position=(base, 0)),
                    inc="semT",
                    waits=[expm.get((hb, bank)), ("semV", qk_ready_v),
                           ("semS", qk_ready_s), ("semV", v_ready)])
                st.setdefault("stm", {})[(hb, kc)] = B.n("semT")

        for pair in range(grp * 4, grp * 4 + 4):
            for s in range(NSPAN):
                nkc = 4 * s + 4
                emit_st(pair, s, 0)
                emit_st(pair, s, 1)
                for kc0 in range(0, nkc, 2):
                    if kc0 + 2 < nkc:
                        emit_st(pair, s, kc0 + 2)
                    b0, b1 = kc0 % 3, (kc0 + 1) % 3
                    s0, s1 = kc0 % 2, (kc0 + 1) % 2
                    band = (kc0 + 1) - 4 * s >= 0
                    for hb in (0, 1):
                        pst = PSA if hb == 0 else PSB
                        ew = [("semT", st["stm"][(hb, kc0 + 1)]),
                              ("semT", pvtm.get((hb, s0))),
                              ("semT", pvtm.get((hb, s1)))]
                        if not band:
                            pin = bass.AP(pst.ap().tensor, b0 * 512,
                                          [[1536, 128],
                                           [(b1 - b0) * 512, 2], [1, 512]])
                            pout = bass.AP(PT.ap().tensor,
                                           hb * 1024 + s0 * 512,
                                           [[2048, 128],
                                            [(s1 - s0) * 512, 2], [1, 512]])
                            if l0:
                                B.emit("S", lambda sc, pin=pin, pout=pout:
                                       sc.activation(
                                    pout, pin, AF.Exp,
                                    scale=QSCALE0, bias=CB[:, 1:2]),
                                    inc="semS", waits=ew)
                            else:
                                B.emit("S", lambda sc, pin=pin:
                                       sc.activation(
                                    TANH[:, 0:1024], pin, AF.Tanh,
                                    scale=TANH_SC), inc="semS", waits=ew)
                                B.emit("S", lambda sc, pout=pout:
                                       sc.activation(
                                    pout, TANH[:, 0:1024], AF.Exp,
                                    scale=50.0, bias=CB[:, 2:3]),
                                    inc="semS")
                            mk = ("semS", B.n("semS"))
                            expm[(hb, b0)] = mk
                            expm[(hb, b1)] = mk
                        else:
                            for kc, bank, slot in ((kc0, b0, s0),
                                                   (kc0 + 1, b1, s1)):
                                j = kc - 4 * s
                                off = 128 * j if j > 0 else 0
                                psst = pst[:, bank * 512:(bank + 1) * 512]
                                if l0:
                                    B.emit("S", lambda sc, psst=psst, hb=hb,
                                           slot=slot, off=off: sc.activation(
                                        PT[:, hb, slot, off:512],
                                        psst[:, off:512], AF.Exp,
                                        scale=QSCALE0, bias=CB[:, 1:2]),
                                        inc="semS", waits=ew)
                                else:
                                    B.emit("S", lambda sc, psst=psst, off=off:
                                           sc.activation(
                                        TANH[:, off:512], psst[:, off:512],
                                        AF.Tanh, scale=TANH_SC),
                                        inc="semS", waits=ew)
                                    B.emit("S", lambda sc, hb=hb, slot=slot,
                                           off=off: sc.activation(
                                        PT[:, hb, slot, off:512],
                                        TANH[:, off:512], AF.Exp,
                                        scale=50.0, bias=CB[:, 2:3]),
                                        inc="semS")
                                expm[(hb, bank)] = ("semS", B.n("semS"))
                        if hb == 1 and kc0 + 3 < nkc:
                            emit_st(pair, s, kc0 + 3)
                        for kc, bank, slot in ((kc0, b0, s0),
                                               (kc0 + 1, b1, s1)):
                            j = kc - 4 * s
                            off = 128 * j if j > 0 else 0
                            nmask = None
                            if j >= 0:
                                if off > 0:
                                    B.emit("V", lambda v, hb=hb, slot=slot,
                                           off=off: v.memset(
                                        PT[:, hb, slot, 0:off], 0.0),
                                        inc="semV",
                                        waits=[expm[(hb, bank)],
                                               ("semT", pvtm.get((hb, slot)))])
                                B.emit("V", lambda v, hb=hb, slot=slot,
                                       off=off: v.tensor_tensor(
                                    PT[:, hb, slot, off:off + 128],
                                    PT[:, hb, slot, off:off + 128],
                                    MSK[:, 512:640], op=ALU.mult),
                                    inc="semV", waits=[expm[(hb, bank)]])
                                nmask = B.n("semV")
                            pvp = PSC if hb == 0 else PSD
                            lh = pair * 2 + hb - grp * 8
                            wpv = [expm[(hb, bank)]]
                            if nmask:
                                wpv.append(("semV", nmask))
                            if kc == 0 and pv_last is not None:
                                wpv.append(("semV", pv_last))
                            B.emit("T", lambda t, kc=kc, lh=lh, s=s, pvp=pvp,
                                   hb=hb, slot=slot: t.matmul(
                                pvp[:, :], VO[:, kc, lh, 0:128],
                                PT[:, hb, slot, :],
                                start=(kc == 0), stop=(kc == 4 * s + 3)),
                                inc="semT", waits=wpv)
                            pvtm[(hb, slot)] = B.n("semT")
                sl = slice(s * SPW, (s + 1) * SPW)
                par = (pair * NSPAN + s) % 2
                B.emit("V", lambda v, pair=pair, sl=sl: v.tensor_copy(
                    QT[0:64, pair, sl], PSC[0:64, :]),
                    inc="semV", waits=[("semT", pvtm[(0, (4 * s + 3) % 2)]),
                                       ("dml", drow_dma.get(par))])
                B.emit("V", lambda v, par=par: v.tensor_copy(
                    DROW[0:1, par, :], PSC[64:65, :]), inc="semV")
                B.emit("V", lambda v, pair=pair, sl=sl: v.tensor_copy(
                    QT[64:128, pair, sl], PSD[0:64, :]),
                    inc="semV", waits=[("semT", pvtm[(1, (4 * s + 3) % 2)])])
                B.emit("V", lambda v, par=par: v.tensor_copy(
                    DROW[32:33, par, :], PSD[64:65, :]), inc="semV")
                pv_last = B.n("semV")
                B.emit("G", lambda g, pair=pair, sl=sl, par=par: g.dma_start(
                    out=denh[2 * pair:2 * pair + 1, sl],
                    in_=DROW[0:1, par, :]),
                    inc="dml", k=16, waits=[("semV", pv_last)])
                B.emit("G", lambda g, pair=pair, sl=sl, par=par: g.dma_start(
                    out=denh[2 * pair + 1:2 * pair + 2, sl],
                    in_=DROW[32:33, par, :]),
                    inc="dml", k=16)
                drow_dma[par] = B.n("dml")
        st["pv_last"] = pv_last
        return B.n("semV")

    def attention_finish(st, QT):
        pv_last = st["pv_last"]
        B.emit("G", lambda g: g.dma_start(out=STAT[0:16, :], in_=denh),
               inc="dml", k=16, waits=[("dml", B.n("dml"))])
        nrb = B.n("dml")
        B.emit("S", lambda sc: sc.activation(STAT[0:16, :], STAT[0:16, :],
                                             AF.Ln),
               inc="semS", waits=[("dml", nrb)])
        B.emit("S", lambda sc: sc.activation(RECB[0:16, :], STAT[0:16, :],
                                             AF.Exp, scale=-1.0), inc="semS")
        nrec = B.n("semS")
        emk = {0: None, 1: None}
        for pair in range(NCHUNK):
            for s in range(NSPAN):
                sl = slice(s * SPW, (s + 1) * SPW)
                par = (pair * NSPAN + s) % 2
                ps = PSC if par == 0 else PSD
                B.emit("T", lambda t, pair=pair, sl=sl, ps=ps: t.matmul(
                    ps[:, :], PATS[:, pair, :], RECB[0:16, sl],
                    start=True, stop=True),
                    inc="semT",
                    waits=[("semS", nrec), ("semV", pv_last), emk[par]])
                nb2 = B.n("semT")
                B.emit("S", lambda sc, sl=sl, ps=ps, par=par: sc.activation(
                    ATMP[:, par * SPW:(par + 1) * SPW], ps[:, :],
                    AF.Identity),
                    inc="semS", waits=[("semT", nb2), emk[par]])
                nbe = B.n("semS")
                B.emit("V", lambda v, pair=pair, sl=sl, par=par:
                       v.tensor_tensor(
                    QT[:, pair, sl], QT[:, pair, sl],
                    ATMP[:, par * SPW:(par + 1) * SPW], op=ALU.mult),
                    inc="semV", waits=[("semS", nbe)])
                emk[par] = ("semV", B.n("semV"))
        return B.n("semV")

    init_es.close()
    def _phases():
        ln1_marks = stream_ln(X, XT_)
        ln1_done = ln1_marks[-1]
        ck('ln1', XT_)

        l0_es = ExitStack()
        open_scopes.append(l0_es)
        A0 = l0_es.enter_context
        QT = A0(nc.sbuf_tensor("QT0", [128, NCHUNK, T], BF16))
        KT = A0(nc.sbuf_tensor("KT0", [128, NCHUNK, T], BF16))
        VO = A0(nc.sbuf_tensor("VO0", [128, 16, 8, 128], BF16))
        WVF = A0(nc.sbuf_tensor("WVF0", [128, NCHUNK, 256], BF16))

        sQ = projection(wq0, XT_, "q0", moving_ready=ln1_marks, bias_col=0, dst=QT,
                        dma_guard=True)
        ck('q_raw', QT)
        q_roped = rope_apply(QT, sQ)
        ck('q_roped', QT)
        sK = projection(wk0, XT_, "k0", bias_col=8, dst=KT, no_sync=True,
                        mm_first_waits=[("semS", sQ)])
        k_roped = rope_apply(KT, sK)
        ast0 = {}
        v_done = v_projection(WVF, wv0, XT_, ln1_done, True, VO, 0,
                              no_sync=True, kv_guard=k_roped,
                              first_waits=[("semS", sK)])
        attention_core(ast0, True, QT, KT, VO, 0, v_done,
                       max(q_roped, k_roped), None)
        v_done = v_projection(WVF, wv0, XT_, ln1_done, True, VO, 1)
        attention_core(ast0, True, QT, KT, VO, 1, v_done,
                       max(q_roped, k_roped), None)
        ck('attn_raw', QT)
        attn0_done = attention_finish(ast0, QT)
        ck('attn0', QT)
        Y = XT_
        wo0m = []
        projection(wo0, QT, "o0", moving_ready=attn0_done, bias_col=24, dst=Y,
                   spanm_out=wo0m)
        ck('wo0', Y)
        ln2span = inner_ln_residual(Y, src_marks=wo0m)
        ck('res1', X)
        B.flush()
        l0_es.close()
        open_scopes.remove(l0_es)
        B.flush()
        trig_es.close()
        open_scopes.remove(trig_es)

        ln3_marks = stream_ln(X, XT_, skip_sum=True,
                              src_marks=[("semV", m) for m in ln2span])
        ln3_done = ln3_marks[-1]
        f0_es = ExitStack()
        open_scopes.append(f0_es)
        A1 = f0_es.enter_context
        HSP = A1(nc.sbuf_tensor("HSP", [128, 24, 1024], BF16))
        WGS = A1(nc.sbuf_tensor("WGS", [128, 2, NCHUNK, 128], BF16))
        WUS = A1(nc.sbuf_tensor("WUS", [128, 2, NCHUNK, 128], BF16))
        WDS = A1(nc.sbuf_tensor("WDS", [128, 2, 24, 128], BF16))
        GTMP = A1(nc.sbuf_tensor("GTMP", [128, 4, 512], BF16))
        wg_r = wg_h.rearrange("(c p) f -> p c f", p=128)
        wu_r = wu_h.rearrange("(c p) f -> p c f", p=128)
        wd_r = wd_h.rearrange("(c p) f -> p c f", p=128)

        bank_map = [(PSA, 0), (PSA, 1), (PSA, 2), (PSB, 0),
                    (PSB, 1), (PSB, 2), (PSC, None), (PSD, None)]

        def bank(i):
            tns, off = bank_map[i % 8]
            if off is None:
                return tns[:, :]
            return tns[:, off * 512:(off + 1) * 512]

        phase_sync()
        guard0 = now_sv()
        ffn0m = {}
        for S_ in range(2):
            hmarks = {}
            first_S = True
            for hM in range(24):
                dw = [("semT", hmarks.get(hM - 2))]
                if S_ == 0 and hM == 0:
                    dw += guard0
                B.emit("G", dma(WGS[:, hM % 2, :, :],
                                wg_r[:, :, hM * 128:(hM + 1) * 128]),
                       inc="dml", k=16, waits=dw)
                B.emit("G", dma(WUS[:, hM % 2, :, :],
                                wu_r[:, :, hM * 128:(hM + 1) * 128]), inc="dml", k=16)
                nw = B.n("dml")
                for ss in range(2):
                    tok0 = S_ * 1024 + ss * 512
                    gbank = bank(ss * 2 + (hM % 2))
                    ubank = bank(4 + ss * 2 + (hM % 2))
                    wts = [("dml", nw), ("semV", ln3_done),
                           ("semV", hmarks.get((hM - 2, ss))),
                           ("semS", hmarks.get((hM - 2, ss, "s")))]
                    if first_S:
                        wts += now_sv()
                        first_S = False
                    for c in range(NCHUNK):
                        B.emit("T", lambda t, c=c, hM=hM, tok0=tok0, gbank=gbank:
                               t.matmul(
                            gbank, WGS[:, hM % 2, c, :],
                            XT_[:, c, tok0:tok0 + 512],
                            start=(c == 0), stop=(c == NCHUNK - 1)),
                            inc="semT", waits=wts if c == 0 else ())
                    for c in range(NCHUNK):
                        B.emit("T", lambda t, c=c, hM=hM, tok0=tok0, ubank=ubank:
                               t.matmul(
                            ubank, WUS[:, hM % 2, c, :],
                            XT_[:, c, tok0:tok0 + 512],
                            start=(c == 0), stop=(c == NCHUNK - 1)), inc="semT")
                    nmm = B.n("semT")
                    gt = GTMP[:, ss * 2 + hM % 2, :]
                    B.emit("S", lambda sc, gt=gt, gbank=gbank: sc.activation(
                        gt, gbank, AF.Silu, scale=DEQ),
                        inc="semS",
                        waits=[("semT", nmm), ("semV", hmarks.get((hM - 2, ss)))])
                    nsl = B.n("semS")
                    hmarks[(hM, ss, "s")] = nsl
                    B.emit("V", lambda v, hM=hM, ss=ss, gt=gt, ubank=ubank:
                           v.tensor_tensor(
                        HSP[:, hM, ss * 512:(ss + 1) * 512], gt, ubank,
                        op=ALU.mult), inc="semV", waits=[("semS", nsl)])
                    hmarks[(hM, ss)] = B.n("semV")
                hmarks[hM] = B.n("semT")
            h_done = B.n("semV")
            dmarks = {}
            for d in range(NCHUNK):
                B.emit("G", dma(WDS[:, d % 2, :, :],
                                wd_r[:, :, d * 128:(d + 1) * 128]),
                       inc="dml", k=16, waits=[("semT", dmarks.get(d - 2))])
                nw = B.n("dml")
                for ss in range(2):
                    tok0 = S_ * 1024 + ss * 512
                    dbank = bank(ss * 2 + (d % 2))
                    wts = [("dml", nw), ("semV", h_done),
                           ("semS", dmarks.get((d - 2, ss)))]
                    for hM in range(24):
                        B.emit("T", lambda t, hM=hM, d=d, ss=ss, dbank=dbank:
                               t.matmul(
                            dbank, WDS[:, d % 2, hM, :],
                            HSP[:, hM, ss * 512:(ss + 1) * 512],
                            start=(hM == 0), stop=(hM == 23)),
                            inc="semT", waits=wts if hM == 0 else ())
                    nmm = B.n("semT")
                    B.emit("S", lambda sc, d=d, tok0=tok0, dbank=dbank:
                           sc.activation(
                        XT_[:, d, tok0:tok0 + 512], dbank, AF.Identity,
                        scale=DEQ * DEQ), inc="semS", waits=[("semT", nmm)])
                    dmarks[(d, ss)] = B.n("semS")
                    if d == NCHUNK - 1:
                        ffn0m[S_ * 2 + ss] = ("semS", dmarks[(d, ss)])
                dmarks[d] = B.n("semT")
        ck('ffn0', XT_)
        B.flush()
        f0_es.close()
        open_scopes.remove(f0_es)
        ln4span = inner_ln_residual(XT_,
                                    src_marks=[ffn0m[3]] * 4)
        ln4_done = ln4span[-1]
        ck('res2', X)

        ln5_marks = stream_ln(X, XT_, skip_sum=True,
                              src_marks=[("semV", m) for m in ln4span])
        ln5_done = ln5_marks[-1]
        l1_es = ExitStack()
        open_scopes.append(l1_es)
        A2 = l1_es.enter_context
        QT = A2(nc.sbuf_tensor("QT1", [128, NCHUNK, T], BF16))
        KT = A2(nc.sbuf_tensor("KT1", [128, NCHUNK, T], BF16))
        VO = A2(nc.sbuf_tensor("VO1", [128, 16, 8, 128], BF16))
        WVF = A2(nc.sbuf_tensor("WVF1", [128, NCHUNK, 256], BF16))
        TANH = A2(nc.sbuf_tensor("TANH", [128, 1024], F32))

        sQ1 = projection(wq1, XT_, "q1", moving_ready=ln5_marks, dst=QT,
                         dma_guard=True)
        sK1 = projection(wk1, XT_, "k1", dst=KT, no_sync=True,
                         mm_first_waits=[("semS", sQ1)])
        ast1 = {}
        v1_done = v_projection(WVF, wv1, XT_, ln5_done, False, VO, 0,
                              no_sync=True, first_waits=[("semS", sK1)])
        attention_core(ast1, False, QT, KT, VO, 0, v1_done, None,
                       max(sQ1, sK1), TANH=TANH)
        v1_done = v_projection(WVF, wv1, XT_, ln5_done, False, VO, 1)
        attention_core(ast1, False, QT, KT, VO, 1, v1_done, None,
                       max(sQ1, sK1), TANH=TANH)
        attn1_done = attention_finish(ast1, QT)
        Y = XT_
        wo1m = []
        projection(wo1, QT, "o1", moving_ready=attn1_done, dst=Y,
                   spanm_out=wo1m)
        ck('wo1', Y)
        ln6span = inner_ln_residual(Y, src_marks=wo1m)
        ck('res3', X)
        B.flush()
        l1_es.close()
        open_scopes.remove(l1_es)

        ln7_marks = stream_ln(X, XT_, skip_sum=True,
                              src_marks=[("semV", m) for m in ln6span])
        ln7_done = ln7_marks[-1]
        f1_es = ExitStack()
        open_scopes.append(f1_es)
        A3 = f1_es.enter_context
        BUF1 = A3(nc.sbuf_tensor("BUF1", [128, NCHUNK, T], BF16))
        RR = A3(nc.sbuf_tensor("RR", [128, NCHUNK, T], BF16))
        KREL = A3(nc.sbuf_tensor("KREL", [128, NCHUNK, T], BF16))
        XP = A3(nc.sbuf_tensor("XP", [128, T], BF16))

        def ts_xp(s, c, rdy_marks):
            if s == 0:
                B.emit("V", lambda v: v.memset(XP[:, 0:1], 0.0),
                       inc="semV", waits=[("semV", rdy_marks[0])])
                B.emit("V", lambda v, c=c: v.tensor_copy(
                    XP[:, 1:SPW], XT_[:, c, 0:SPW - 1]), inc="semV")
                return XP[:, 0:SPW], []
            return (XT_[:, c, s * SPW - 1:(s + 1) * SPW - 1],
                    [("semV", rdy_marks[s])])

        def token_shift_r(rdy_marks):
            marks = []
            for s in range(NSPAN):
                sl = slice(s * SPW, (s + 1) * SPW)
                for c in range(NCHUNK):
                    xp, w0 = ts_xp(s, c, rdy_marks)
                    B.emit("V", lambda v, c=c, sl=sl, xp=xp: v.tensor_tensor(
                        KREL[:, c, sl], XT_[:, c, sl], xp,
                        op=ALU.subtract), inc="semV", waits=w0)
                    B.emit("V", lambda v, c=c, sl=sl, xp=xp:
                           v.scalar_tensor_tensor(
                        BUF1[:, c, sl], KREL[:, c, sl],
                        VEC[:, 32 + c:32 + c + 1],
                        xp, op0=ALU.mult, op1=ALU.add), inc="semV")
                marks.append(B.n("semV"))
            return marks

        def token_shift_k(rdy_marks):
            marks = []
            for s in range(NSPAN):
                sl = slice(s * SPW, (s + 1) * SPW)
                for c in range(NCHUNK):
                    xp, w0 = ts_xp(s, c, rdy_marks)
                    B.emit("V", lambda v, c=c, sl=sl, xp=xp:
                           v.scalar_tensor_tensor(
                        KREL[:, c, sl], KREL[:, c, sl],
                        VEC[:, 40 + c:40 + c + 1],
                        xp, op0=ALU.mult, op1=ALU.add), inc="semV", waits=w0)
                marks.append(B.n("semV"))
            return marks

        xrm = token_shift_r(ln7_marks)
        xkm = token_shift_k(ln7_marks)
        sR = projection(wr1, BUF1, "r1", moving_ready=xrm[-1], act=AF.Sigmoid,
                        dst=RR, dma_guard=True)
        sKk = projection(wk1f, KREL, "kk1", moving_ready=xkm[-1], act=AF.Relu,
                         dst=BUF1, no_sync=True,
                         mm_first_waits=[("semS", sR)])
        for c in range(NCHUNK):
            B.emit("V", lambda v, c=c: v.tensor_tensor(
                BUF1[:, c, :], BUF1[:, c, :], BUF1[:, c, :], op=ALU.mult),
                inc="semV", waits=[("semS", sKk)])
        ka_done = B.n("semV")

        def kv_evac(f, s, ps, nmm):
            B.emit("V", lambda v, f=f, s=s, ps=ps: v.tensor_tensor(
                XT_[:, f, s * SPW:(s + 1) * SPW], ps,
                RR[:, f, s * SPW:(s + 1) * SPW], op=ALU.mult),
                inc="semV", waits=[("semT", nmm), ("semS", sR)])
            return B.n("semV")

        kv1m = []
        projection(wv1f, BUF1, "kv1", evac=kv_evac, moving_ready=ka_done,
                   no_sync=True, mm_first_waits=[("semS", sKk)],
                   spanm_out=kv1m)
        ck('ffn1', XT_)
        B.flush()
        f1_es.close()
        open_scopes.remove(f1_es)
        ln8_done = inner_ln_residual(XT_, src_marks=kv1m)[-1]


    try:
        _phases()
    except _StopEmit:
        pass
    B.flush()
    for sc_ in reversed(open_scopes):
        sc_.close()
    if stop_state:
        out_src, mv, ms = stop_state[0]
    else:
        out_src, mv, ms = X, B.n("semV"), B.n("semS")
    B.emit("G", lambda g: g.dma_start(
        out=out_h.rearrange("(c p) t -> p c t", p=128), in_=out_src[:, :, :]),
        inc="dms", k=16, waits=[("semV", mv), ("semS", ms)])
    B.emit("G", lambda g: g.nop(), waits=[("dms", B.n("dms"))])
    B.flush()

    blk.__exit__(None, None, None)
    es.close()
    return nc


def _shard_inputs(inputs):
    x = np.ascontiguousarray(np.asarray(inputs["L0_attn_x"], np.float32))
    vecs = np.zeros((128, 48), dtype=np.float32)
    for i, nm in enumerate(["L0_attn_bq", "L0_attn_bk", "L0_attn_bv",
                            "L0_attn_bo"]):
        vecs[:, i * 8:(i + 1) * 8] = np.asarray(
            inputs[nm], np.float32).reshape(8, 128).T
    vecs[:, 32:40] = np.asarray(inputs["L1_ffn_mu_r"],
                                np.float32).reshape(8, 128).T
    vecs[:, 40:48] = np.asarray(inputs["L1_ffn_mu_k"],
                                np.float32).reshape(8, 128).T
    rows4 = np.asarray(inputs["L0_attn_bv"], np.float32).reshape(1, D)

    def tr(a, dt=np.float32):
        return np.ascontiguousarray(np.asarray(a, dt).T)

    shared = {
        "wq0": tr(inputs["L0_attn_Wq"]), "wk0": tr(inputs["L0_attn_Wk"]),
        "wv0": tr(inputs["L0_attn_Wv"]), "wo0": tr(inputs["L0_attn_Wo"]),
        "vecs": vecs, "rows4": np.ascontiguousarray(rows4),
        "wg": tr(inputs["L0_ffn_w_gate"], np.int32),
        "wu": tr(inputs["L0_ffn_w_up"], np.int32),
        "wd": tr(inputs["L0_ffn_w_down"], np.int32),
        "wq1": tr(inputs["L1_attn_Wq"]), "wk1": tr(inputs["L1_attn_Wk"]),
        "wv1": tr(inputs["L1_attn_Wv"]), "wo1": tr(inputs["L1_attn_Wo"]),
        "wr1": tr(inputs["L1_ffn_W_r"]), "wk1f": tr(inputs["L1_ffn_W_k"]),
        "wv1f": tr(inputs["L1_ffn_W_v"]),
    }
    pats = np.zeros((16, 1024), np.float32)
    for p in range(8):
        pats[2 * p, p * 128:p * 128 + 64] = 1
        pats[2 * p + 1, p * 128 + 64:p * 128 + 128] = 1
    msk = (np.arange(1024)[None, :] >= np.arange(128)[:, None] + 512)
    psw = np.zeros((128, 128), np.float32)
    psw[np.arange(128), np.arange(128) ^ 32] = 1
    cpos = np.broadcast_to(np.arange(1, 2049, dtype=np.float32)[None, :],
                           (128, 2048))
    dmod = (np.arange(128) & 31).astype(np.float32)
    sgn = np.where((np.arange(128) & 32) != 0, 1.0, -1.0).astype(np.float32)
    shared["cpats"] = pats
    shared["cmsk"] = np.ascontiguousarray(msk.astype(np.float32))
    shared["cpsw"] = psw
    shared["cpos"] = np.ascontiguousarray(cpos)
    shared["cdd"] = np.ascontiguousarray(np.stack([dmod, sgn], axis=1))
    in_maps = []
    for b in range(8):
        m = dict(shared)
        m["xT"] = np.ascontiguousarray(x[b].T)
        in_maps.append(m)
    return in_maps


def kernel(**inputs):
    in_maps = _shard_inputs(inputs)
    nc = build_nc()
    res = run_bass_kernel_spmd(nc, in_maps, core_ids=list(range(8)))
    out = np.stack([np.ascontiguousarray(r["out"].T) for r in res.results])
    return out
